# version 10
# baseline (speedup 1.0000x reference)
import os
import numpy as np
import concourse.bass as bass
import concourse.mybir as mybir
from concourse.bass_utils import run_bass_kernel_spmd

F32 = mybir.dt.float32
BF16 = mybir.dt.bfloat16
U32 = mybir.dt.uint32
AF = mybir.ActivationFunctionType
ALU = mybir.AluOpType
AX = mybir.AxisListType

D = 1024
G = 8
TOK = 2048
NTILE = 16
NTG = 16
NB = 4
TGB = 4
HALF = 1024


def _skey(sem):
    return getattr(sem, "name", None) or id(sem)


class Walk:

    def __init__(self, mode, events=None):
        self.mode = mode
        self.events = events if events is not None else []
        self.n = 0
        self.counts = {}
        self.waited = {}
        self.eng = None

    def op(self, eng_key, emit_fn, waits=(), inc=None, inc_n=1):
        ev = None
        if inc is not None:
            ev = self.n
            self.n += 1
        if self.mode == "count":
            if inc is not None:
                k = _skey(inc)
                self.counts[k] = self.counts.get(k, 0) + inc_n
                self.events.append((inc, self.counts[k]))
        elif eng_key == self.mode:
            for w in waits:
                if w is None:
                    continue
                sem, val = self.events[w]
                k = _skey(sem)
                if self.waited.get(k, 0) < val:
                    self.eng.wait_ge(sem, val)
                    self.waited[k] = val
            instr = emit_fn(self.eng)
            if inc is not None and instr is not None:
                instr.then_inc(inc, inc_n)
        return ev


def build_program():
    nc = bass.Bass("TRN2", target_bir_lowering=False)

    def inp(name, shape, dt=F32):
        return nc.dram_tensor(name, shape, dt, kind="ExternalInput").ap()

    xs = inp("xs", [TOK, D])
    win = inp("win", [G, 128, 128])
    wa = inp("wa", [G, 128, 128])
    bin_t = inp("bin_t", [128, G])
    ba_t = inp("ba_t", [128, G])
    apar_t = inp("apar_t", [128, G])
    Wgd = inp("Wgd", [D, D])
    Wed = inp("Wed", [G, D, D])
    bgd = inp("bgd", [1, D])
    bed = inp("bed", [G, D])
    ident = inp("ident", [128, 128])
    msel = inp("msel", [8, 8])
    iot8 = inp("iot8", [128, 8])
    ys = nc.dram_tensor("ys", [TOK, D], F32, kind="ExternalOutput").ap()

    cc_in = nc.dram_tensor("cc_in", [2, D], F32).ap()
    cc_out = nc.dram_tensor("cc_out", [16, D], F32).ap()
    we16 = nc.dram_tensor("we16", [G, 128, G * D], BF16).ap()
    wg16 = nc.dram_tensor("wg16", [128, G * D], BF16).ap()

    from contextlib import ExitStack
    stack = ExitStack()
    sb = lambda n, s, d=F32: stack.enter_context(nc.sbuf_tensor(n, s, d))

    UN = sb("UN", [128, 2048])
    NXb = sb("NXb", [128, G * TOK])
    ABIG = sb("ABIG", [128, G * TOK])
    Rb = sb("Rb", [128, 45056], mybir.dt.uint8)
    id_f = sb("id_f", [128, 128])
    id_b = sb("id_b", [128, 128], BF16)
    win_b = sb("win_b", [128, G * 128], BF16)
    wa_b = sb("wa_b", [128, G * 128], BF16)
    binh = sb("binh", [128, G])
    bah = sb("bah", [128, G])
    apar = sb("apar", [128, G])
    scl = sb("scl", [128, G])
    sclh = sb("sclh", [128, G])
    sclb = sb("sclb", [128, G])
    spt = sb("spt", [128, G])
    spt2 = sb("spt2", [128, G])
    gaS = sb("gaS", [128, 16])
    gaQ = sb("gaQ", [128, 32])
    msl1m = sb("msl1m", [8, 8])
    Pha = sb("Pha", [128, 16])
    Fha = sb("Fha", [128, 16])
    PcFc = sb("PcFc", [128, 16])
    ccsb = sb("ccsb", [8, 256])
    msl = sb("msl", [8, 8])
    Htile = sb("Htile", [8, 128])
    Ht = sb("Ht", [128, 8])
    Gt = sb("Gt", [128, 16])
    t8a = sb("t8a", [8, 128])
    t8b = sb("t8b", [8, 128])
    iota = sb("iota", [128, 8])
    bgf = sb("bgf", [1, D])
    BGfull = sb("BGfull", [128, D])
    ones1 = sb("ones1", [1, 128])
    alphaT = sb("alphaT", [8, TGB * 128], BF16)
    behi16 = sb("behi16", [8, D], BF16)
    belo16 = sb("belo16", [8, D], BF16)
    alpha = sb("alpha", [128, 2 * 32])
    idx4 = sb("idx4", [128, 32], U32)
    e8u4 = sb("e8u4", [128, 4], U32)
    e8f4 = sb("e8f4", [128, 4])
    me8 = sb("me8", [128, 8])
    nmz4 = sb("nmz4", [128, 4])
    mzt4 = sb("mzt4", [128, 4])
    ssum4 = sb("ssum4", [128, 4])
    p4 = sb("p4", [128, 4])
    stM = sb("stM", [128, NTG])
    stR = sb("stR", [128, NTG])
    stQ = sb("stQ", [128, NTG])
    stQ2 = sb("stQ2", [128, NTG])
    stQh = sb("stQh", [128, 8])
    xnb = sb("xnb", [128, 2 * D], BF16)

    rb8 = Rb.ap()

    def rview(off_b, p, fsz, dt):
        return rb8[0:p, off_b:off_b + fsz * mybir.dt.size(dt)].bitcast(dt)

    xA4 = [rview(0, 128, 4 * D, F32), rview(16384, 128, 4 * D, F32)]
    xbg_ = [rview(32768, 128, 512, BF16), rview(33792, 128, 512, BF16)]
    xf_ = [rview(34816 + 2048 * i, 128, 512, F32) for i in range(3)]
    winf = rview(32768, 128, G * 128, F32)
    waf = rview(36864, 128, G * 128, F32)
    We_ = [rview(0, 128, G * D, BF16), rview(16384, 128, G * D, BF16)]
    stg_ = [rview(32768, 128, D, F32), rview(36864, 128, D, F32)]
    beds = rview(40960, 8, D, F32)

    ab8 = ABIG.ap().bitcast(mybir.dt.uint8)

    def aview(off_b, p, fsz, dt):
        return ab8[0:p, off_b:off_b + fsz * mybir.dt.size(dt)].bitcast(dt)

    lhsT_ = [aview(0, 128, 4 * D, BF16), aview(8192, 128, 4 * D, BF16)]
    ZS_ = [aview(16384, 128, 4 * D, F32), aview(32768, 128, 4 * D, F32)]
    xq_ = [aview(49152 + 4096 * i, 128, D, F32) for i in range(4)]
    scrapL = [aview(0, 128, 2 * D, F32), aview(8192, 128, 2 * D, F32)]

    psT = stack.enter_context(nc.psum_tensor("psT", [128, 1024], F32))
    psB = stack.enter_context(nc.psum_tensor("psB", [128, 1024], BF16))
    psM = stack.enter_context(nc.psum_tensor("psM", [128, 512], F32))
    PS = [stack.enter_context(nc.psum_tensor("PS0", [128, 1024], F32)),
          stack.enter_context(nc.psum_tensor("PS1", [128, 1024], F32))]

    s_ld = stack.enter_context(nc.semaphore("s_ld"))
    s_x0 = stack.enter_context(nc.semaphore("s_x0"))
    s_x1 = stack.enter_context(nc.semaphore("s_x1"))
    s_w0 = stack.enter_context(nc.semaphore("s_w0"))
    s_w1 = stack.enter_context(nc.semaphore("s_w1"))
    s_we0 = stack.enter_context(nc.semaphore("s_we0"))
    s_we1 = stack.enter_context(nc.semaphore("s_we1"))
    s_wb = stack.enter_context(nc.semaphore("s_wb"))
    s_o0 = stack.enter_context(nc.semaphore("s_o0"))
    s_o1 = stack.enter_context(nc.semaphore("s_o1"))
    s_cd = stack.enter_context(nc.semaphore("s_cd"))
    s_dbg = stack.enter_context(nc.semaphore("s_dbg"))
    s_cc = stack.enter_context(nc.semaphore("s_cc"))
    s_v = stack.enter_context(nc.semaphore("s_v"))
    s_a = stack.enter_context(nc.semaphore("s_a"))
    s_p = stack.enter_context(nc.semaphore("s_p"))
    s_g = stack.enter_context(nc.semaphore("s_g"))

    PFg = UN.ap()[0:8, :]
    gxb = [UN.ap()[:, 0:512], UN.ap()[:, 512:1024]]
    gab = [UN.ap()[:, 1024:1536], UN.ap()[:, 1536:2048]]
    xnbf = xnb.ap().bitcast(F32)
    gaQv = gaQ.ap().rearrange("p (g c) -> p g c", c=4)
    NX3 = NXb.ap().rearrange("p (g t) -> p g t", t=TOK)
    AB3 = ABIG.ap().rearrange("p (g t) -> p g t", t=TOK)
    gaS3 = gaS.ap().rearrange("p (g h) -> p g h", h=2)
    Pv = Pha.ap().rearrange("p (g h) -> p h g", h=2)
    Fv = Fha.ap().rearrange("p (g h) -> p h g", h=2)
    Gv = Gt.ap().rearrange("p (g h) -> p h g", h=2)
    Phav = Pha.ap().rearrange("p (g h) -> p g h", h=2)

    def ACC(tg):
        return NX3[:, :, tg * 128:(tg + 1) * 128]

    STAGE = int(os.environ.get("KSTAGE", "9"))
    ys128 = ys.rearrange("(p r) c -> p (r c)", p=128)

    def schedule(W):
        V = lambda f, waits=(), inc=s_v, n=1: W.op("vector", f, waits, inc, n)
        A_ = lambda f, waits=(), inc=s_a, n=1: W.op("scalar", f, waits, inc, n)
        P_ = lambda f, waits=(), inc=s_p, n=1: W.op("tensor", f, waits, inc, n)
        G_ = lambda f, waits=(), inc=s_g, n=1: W.op("gpsimd", f, waits, inc, n)
        GW = lambda f, waits=(), inc=None, n=16: W.op("gpsimd", f, waits, inc, n)
        SY = lambda f, waits=(), inc=None, n=16: W.op("sync", f, waits, inc, n)

        SY(lambda e: e.dma_start(id_f[:, :], ident[:, :]), inc=s_ld)
        SY(lambda e: e.dma_start(
            winf.rearrange("i (g o) -> i g o", o=128),
            win.rearrange("g i o -> i g o")), inc=s_ld)
        SY(lambda e: e.dma_start(
            waf.rearrange("i (g o) -> i g o", o=128),
            wa.rearrange("g i o -> i g o")), inc=s_ld)
        SY(lambda e: e.dma_start(binh[:, :], bin_t[:, :]), inc=s_ld)
        SY(lambda e: e.dma_start(bah[:, :], ba_t[:, :]), inc=s_ld)
        SY(lambda e: e.dma_start(apar[:, :], apar_t[:, :]), inc=s_ld)
        SY(lambda e: e.dma_start(msl[:, :], msel[:, :]), inc=s_ld)
        SY(lambda e: e.dma_start(iota[:, :], iot8[:, :]), inc=s_ld)
        SY(lambda e: e.dma_start(bgf[:, :], bgd[:, :]), inc=s_ld)
        ld_all = SY(lambda e: e.dma_start(beds, bed[:, :]), inc=s_ld)

        c_idb = A_(lambda e: e.copy(id_b[:, :], id_f[:, :]), waits=[ld_all])
        hb1 = V(lambda e: e.tensor_scalar_mul(binh[:, :], binh[:, :], 0.5), waits=[ld_all])
        hb2 = V(lambda e: e.tensor_scalar_mul(bah[:, :], bah[:, :], 0.5), waits=[ld_all])
        c_win = A_(lambda e: e.copy(win_b[:, :], winf), waits=[ld_all])
        c_wa = A_(lambda e: e.copy(wa_b[:, :], waf), waits=[ld_all])
        sp1 = A_(lambda e: e.activation(spt[:, :], apar[:, :], AF.Abs), waits=[ld_all])
        sp2 = A_(lambda e: e.activation(spt[:, :], spt[:, :], AF.Exp, scale=-1.0), waits=[sp1])
        sp3 = A_(lambda e: e.activation(spt[:, :], spt[:, :], AF.Ln, bias=1.0), waits=[sp2])
        sp4 = A_(lambda e: e.activation(scl[:, :], apar[:, :], AF.Relu), waits=[ld_all])
        sp5 = V(lambda e: e.tensor_tensor(scl[:, :], scl[:, :], spt[:, :], ALU.add),
                waits=[sp3, sp4])
        sp6 = V(lambda e: e.tensor_scalar_mul(scl[:, :], scl[:, :], -8.0), waits=[sp5])
        sp7 = V(lambda e: e.tensor_scalar_mul(sclh[:, :], scl[:, :], 0.5), waits=[sp6])
        sp8 = V(lambda e: e.tensor_scalar_mul(sclb[:, :], scl[:, :], 512.0), waits=[sp6])
        z_gaS = V(lambda e: e.memset(gaS[:, :], 0.0))
        z_ones = V(lambda e: e.memset(ones1[:, :], 1.0))
        bh1 = A_(lambda e: e.copy(behi16[:, :], beds), waits=[ld_all])
        bh2 = V(lambda e: e.tensor_tensor(belo16[:, :], beds, behi16[:, :],
                                          ALU.subtract), waits=[bh1])
        be_ready = bh2
        bgm0 = P_(lambda e: e.matmul(psM[:, :], ones1[0:1, :], bgf[0:1, 0:512],
                                     start=True, stop=True), waits=[z_ones, ld_all])
        bge0 = V(lambda e: e.tensor_copy(BGfull[:, 0:512], psM[:, :]), waits=[bgm0])
        bgm1 = P_(lambda e: e.matmul(psM[:, :], ones1[0:1, :], bgf[0:1, 512:1024],
                                     start=True, stop=True), waits=[bge0])
        bge1 = V(lambda e: e.tensor_copy(BGfull[:, 512:1024], psM[:, :]), waits=[bgm1])
        bg_ready = bge1
        xq3 = [q.rearrange("p (g t) -> p g t", t=128) for q in xq_]
        xnb3 = xnb.ap().rearrange("p (s g t) -> p s g t", s=2, t=128)

        xld = [None] * 4
        tp_last = [None] * 4
        psT_c = [[None], [None]]
        xbg_free = [None, None]
        xf_free = [None, None, None]
        gx_free = {}
        ga_free = [None, None]
        ps0_free = [None, None]
        ps1_free = [None, None]
        aexv = {}
        nxfv = {}
        omv = {}
        sxv = {}
        stv = {}
        mma_last = None
        fha_prev = None
        gaSh = [None, None]
        phalf_l = None

        def seg(c, g):
            return NXb[:, g * TOK + c * 512: g * TOK + (c + 1) * 512]

        def aseg(c, g):
            return ABIG[:, g * TOK + c * 512: g * TOK + (c + 1) * 512]

        def emit_snx(u):
            fc, fg = divmod(u, G) if False else (u // G, u % G)
            l1 = A_(lambda e, fc=fc, fg=fg: e.activation(
                seg(fc, fg), seg(fc, fg), AF.Ln, scale=0.25), waits=[omv[u]])
            sxv[u] = A_(lambda e, fc=fc, fg=fg: e.activation(
                seg(fc, fg), seg(fc, fg), AF.Exp, scale=0.5), waits=[l1])

        def emit_ttp(u):
            fc, fg = u // G, u % G
            tt = G_(lambda e, fc=fc, fg=fg, u=u: e.tensor_tensor(
                seg(fc, fg), seg(fc, fg), xf_[u % 3], ALU.mult),
                waits=[sxv[u], stv[u]])
            xf_free[u % 3] = tt
            nxfv[(fc, fg)] = tt
            return tt

        for c in range(4):
            xw = [tp_last[c - 2]] if c >= 2 else []
            xld[c] = SY(lambda e, c=c: e.dma_start(
                xA4[c % 2].rearrange("p (t f) -> p t f", f=D),
                xs[c * 512:(c + 1) * 512, :].rearrange("(t p) f -> p t f", p=128)),
                waits=xw, inc=(s_x0 if c % 2 == 0 else s_x1))
            if c == 2:
                gaSh[0] = V(lambda e: e.tensor_tensor(
                    gaS3[:, :, 0], gaQv[:, :, 0], gaQv[:, :, 1], ALU.add),
                    waits=[aexv[(1, G - 1)]])
            for g in range(G):
                u = c * G + g
                par = g % 2
                wtr = [xld[c], ld_all] + psT_c[par]
                lastT = None
                for i in range(4):
                    lastT = P_(lambda e, c=c, g=g, i=i, par=par: e.transpose(
                        psT[:, par * 512 + i * 128: par * 512 + (i + 1) * 128],
                        xA4[c % 2][:, i * D + g * 128: i * D + (g + 1) * 128],
                        id_f[:, :]), waits=wtr if i == 0 else [])
                if g == G - 1:
                    tp_last[c] = lastT
                web = [lastT]
                if xbg_free[par] is not None:
                    web.append(xbg_free[par])
                if c == 0 and g < 2:
                    web += [c_win, c_wa]
                ebf = V(lambda e, par=par: e.tensor_copy(
                    xbg_[par], psT[:, par * 512:(par + 1) * 512]), waits=web)
                we2 = [lastT]
                if xf_free[u % 3] is not None:
                    we2.append(xf_free[u % 3])
                if c == 0 and g < 3:
                    we2 += [c_win, c_wa]
                ebf2 = V(lambda e, u=u, par=par: e.tensor_copy(
                    xf_[u % 3], psT[:, par * 512:(par + 1) * 512]), waits=we2)
                psT_c[par] = [ebf, ebf2]
                wmx = [ebf, c_win]
                if ps0_free[par] is not None:
                    wmx.append(ps0_free[par])
                mmx = P_(lambda e, g=g, par=par: e.matmul(
                    PS[0][:, par * 512:(par + 1) * 512],
                    win_b[:, g * 128:(g + 1) * 128], xbg_[par],
                    start=True, stop=True), waits=wmx)
                wma = [c_wa]
                if ps1_free[par] is not None:
                    wma.append(ps1_free[par])
                mma = P_(lambda e, g=g, par=par: e.matmul(
                    PS[1][:, par * 512:(par + 1) * 512],
                    wa_b[:, g * 128:(g + 1) * 128], xbg_[par],
                    start=True, stop=True), waits=wma)
                xbg_free[par] = mma
                mma_last = mma
                wgx = [mmx, hb1]
                if u >= 2 and (u - 2) in stv:
                    wgx.append(stv[u - 2])
                tgx = A_(lambda e, g=g, par=par: e.activation(
                    gxb[par], PS[0][:, par * 512:(par + 1) * 512], AF.Tanh,
                    bias=binh[:, g:g + 1], scale=0.5), waits=wgx)
                ps0_free[par] = tgx
                wga = [mma, hb2]
                if ga_free[par] is not None:
                    wga.append(ga_free[par])
                tga = A_(lambda e, g=g, c=c, par=par: e.activation(
                    gab[par], PS[1][:, par * 512:(par + 1) * 512], AF.Tanh,
                    bias=bah[:, g:g + 1], scale=0.5,
                    accum_out=gaQ[:, g * 4 + c: g * 4 + c + 1]), waits=wga)
                ps1_free[par] = tga
                aex = A_(lambda e, g=g, c=c, par=par: e.activation(
                    aseg(c, g), gab[par], AF.Exp,
                    bias=sclh[:, g:g + 1], scale=sclh[:, g:g + 1]), waits=[tga, sp7])
                aexv[(c, g)] = aex
                ga_free[par] = aex
                stv[u] = V(lambda e, u=u, par=par: e.scalar_tensor_tensor(
                    xf_[u % 3], gxb[par], 1.0, xf_[u % 3], ALU.add, ALU.mult),
                    waits=[tgx, ebf2])
                t1 = G_(lambda e, c=c, g=g: e.tensor_tensor(
                    seg(c, g), aseg(c, g), aseg(c, g), ALU.mult), waits=[aex])
                omv[u] = G_(lambda e, c=c, g=g: e.tensor_scalar(
                    seg(c, g), seg(c, g), -1.0, 1.0, ALU.mult, ALU.add), waits=[t1])
                if u % 2 == 1:
                    emit_snx(u - 1)
                    emit_snx(u)
                if u >= 2:
                    emit_ttp(u - 2)
                if c == 2:
                    sc1 = V(lambda e, g=g: e.tensor_tensor_scan(
                        xnbf, ABIG[:, g * TOK: g * TOK + HALF],
                        NXb[:, g * TOK: g * TOK + HALF],
                        0.0, ALU.mult, ALU.add),
                        waits=[nxfv[(1, g)], aexv[(1, g)]]
                        + ([fha_prev] if fha_prev else []))
                    fha_prev = V(lambda e, g=g: e.tensor_copy(
                        Fha[:, g * 2: g * 2 + 1], xnbf[:, HALF - 1:HALF]),
                        waits=[sc1])
        emit_ttp(30)
        emit_ttp(31)
        stageA_done = [mma_last, nxfv[(3, G - 1)], tp_last[3], tp_last[2]]

        if STAGE == 1:
            dump_and_stop(NXb.ap(), stageA_done)
            return

        gaSh[1] = V(lambda e: e.tensor_tensor(
            gaS3[:, :, 1], gaQv[:, :, 2], gaQv[:, :, 3], ALU.add),
            waits=[aexv[(3, G - 1)]])
        for g in range(G):
            sc1 = V(lambda e, g=g: e.tensor_tensor_scan(
                xnbf, ABIG[:, g * TOK + HALF: g * TOK + 2 * HALF],
                NXb[:, g * TOK + HALF: g * TOK + 2 * HALF],
                0.0, ALU.mult, ALU.add),
                waits=[nxfv[(3, g)], aexv[(3, g)], fha_prev])
            fha_prev = V(lambda e, g=g: e.tensor_copy(
                Fha[:, g * 2 + 1: g * 2 + 2], xnbf[:, HALF - 1:HALF]), waits=[sc1])
            phalf_l = A_(lambda e, g=g: e.activation(
                Phav[:, g, :], gaS3[:, g, :], AF.Exp,
                bias=sclb[:, g:g + 1], scale=sclh[:, g:g + 1]),
                waits=[gaSh[1], gaSh[0], sp8] if g == 0 else [phalf_l])
        pass1_done = fha_prev
        phalf_done = phalf_l

        wg_c = [None] * G
        for g in range(G):
            wd = [mma_last, nxfv[(3, G - 1)]] if g < 2 else [wg_c[g - 2]]
            dd = GW(lambda e, g=g: e.dma_start(stg_[g % 2], Wgd[g * 128:(g + 1) * 128, :]),
                    waits=wd, inc=(s_w0 if g % 2 == 0 else s_w1))
            wc = [dd, tp_last[2]]
            wg_c[g] = A_(lambda e, g=g: e.copy(We_[0][:, g * D:(g + 1) * D], stg_[g % 2]),
                         waits=wc if g < 2 else [dd])
        wg_cast_done = wg_c[G - 1]
        wg_wb = SY(lambda e: e.dma_start(wg16[:, :], We_[0][:, :]), waits=[wg_cast_done],
                   inc=s_wb)

        c1 = V(lambda e: e.tensor_tensor(PcFc[:, 0:8], Pv[:, 1, :], Fv[:, 0, :], ALU.mult),
               waits=[pass1_done, phalf_done])
        c2 = V(lambda e: e.tensor_tensor(PcFc[:, 8:16], PcFc[:, 0:8], Fv[:, 1, :], ALU.add),
               waits=[c1])
        c3 = V(lambda e: e.tensor_tensor(PcFc[:, 0:8], Pv[:, 0, :], Pv[:, 1, :], ALU.mult),
               waits=[c2])
        tp1 = P_(lambda e: e.transpose(psM[0:8, 0:128], PcFc[:, 0:8], id_f[:, :]),
                 waits=[c3, bge1])
        tp1e = V(lambda e: e.tensor_copy(ccsb[:, 0:128], psM[0:8, 0:128]), waits=[tp1])
        tp2 = P_(lambda e: e.transpose(psM[0:8, 128:256], PcFc[:, 8:16], id_f[:, :]),
                 waits=[tp1e])
        tp2e = V(lambda e: e.tensor_copy(ccsb[:, 128:256], psM[0:8, 128:256]), waits=[tp2])
        d1 = G_(lambda e: e.dma_start(
            cc_in.rearrange("r (g p) -> g r p", p=128),
            ccsb[:, :].rearrange("g (r p) -> g r p", p=128)),
            waits=[tp1e, tp2e], inc=s_cd, n=16)
        cc = G_(lambda e: e.collective_compute(
            "AllGather", ALU.bypass, replica_groups=[list(range(8))],
            ins=[cc_in.opt()], outs=[cc_out.opt()]), waits=[d1], inc=s_cc, n=1)
        d2 = G_(lambda e: e.dma_start(
            PFg[:, :].rearrange("g (j p) -> g j p", p=128),
            cc_out.rearrange("j (g p) -> g j p", p=128)),
            waits=[cc, mma_last, nxfv[(3, G - 1)]], inc=s_cd, n=16)
        m1m = V(lambda e: e.tensor_scalar(msl1m[:, :], msl[:, :], -1.0, 1.0,
                                          ALU.mult, ALU.add), waits=[ld_all])
        bpf = []
        for j in range(8):
            bpf.append(V(lambda e, j=j: e.tensor_scalar(
                PFg[:, (2 * j) * 128:(2 * j + 1) * 128],
                PFg[:, (2 * j) * 128:(2 * j + 1) * 128],
                msl[:, j:j + 1], msl1m[:, j:j + 1], ALU.mult, ALU.add),
                waits=[d2, m1m]))
            bpf.append(V(lambda e, j=j: e.tensor_scalar(
                PFg[:, (2 * j + 1) * 128:(2 * j + 2) * 128],
                PFg[:, (2 * j + 1) * 128:(2 * j + 2) * 128],
                msl[:, j:j + 1], None, ALU.mult), waits=[d2]))
        prevh = V(lambda e: e.memset(Htile[:, :], 0.0))
        for j in range(8):
            q1 = V(lambda e, j=j: e.tensor_tensor(
                t8a[:, :], PFg[:, (2 * j) * 128:(2 * j + 1) * 128], Htile[:, :],
                ALU.mult), waits=[bpf[2 * j], prevh])
            prevh = V(lambda e, j=j: e.tensor_tensor(
                Htile[:, :], t8a[:, :], PFg[:, (2 * j + 1) * 128:(2 * j + 2) * 128],
                ALU.add), waits=[q1, bpf[2 * j + 1]])
        tp3 = P_(lambda e: e.transpose(psM[:, 256:264], Htile[:, :], id_f[0:8, 0:8]),
                 waits=[prevh, tp2e])
        tp3e = V(lambda e: e.tensor_copy(Ht[:, :], psM[:, 256:264]), waits=[tp3])
        g1 = V(lambda e: e.tensor_copy(Gv[:, 0, :], Ht[:, :]), waits=[tp3e])
        g2 = V(lambda e: e.tensor_tensor(Gv[:, 1, :], Pv[:, 0, :], Ht[:, :], ALU.mult),
               waits=[tp3e, g1])
        g3 = V(lambda e: e.tensor_tensor(Gv[:, 1, :], Gv[:, 1, :], Fv[:, 0, :], ALU.add),
               waits=[g2])

        pass2c = [None] * 4
        prev_cg = [None] * G
        for c in range(4):
            h = c // 2
            for g in range(G):
                base = g * TOK + c * 512
                if c % 2 == 0:
                    init = Gt[:, g * 2 + h: g * 2 + h + 1]
                    wts = [g1, g3] if (c == 0 and g == 0) else ([] if c == 0 else [prev_cg[g]])
                else:
                    init = NXb[:, base - 1: base]
                    wts = [prev_cg[g]]
                prev_cg[g] = V(lambda e, base=base, init=init: e.tensor_tensor_scan(
                    NXb[:, base: base + 512],
                    ABIG[:, base: base + 512],
                    NXb[:, base: base + 512],
                    init, ALU.mult, ALU.add), waits=wts)
            pass2c[c] = prev_cg[G - 1]
        pass2_done = pass2c[3]

        if STAGE == 3:
            dump_and_stop(NXb.ap(), [pass2_done])
            return

        jobs = []
        for t in range(TGB):
            jobs.append(("G", 0, t))
        for b in range(NB):
            for ei in range(3):
                for t in range(TGB):
                    jobs.append(("E", b, ei, t))
            if b < NB - 1:
                for t in range(TGB):
                    jobs.append(("G", b + 1, t))
            for ei in range(3, G):
                for t in range(TGB):
                    jobs.append(("E", b, ei, t))

        wgroups = []
        seen = {}
        for j in jobs:
            key = ("Wg", j[1]) if j[0] == "G" else ("E", j[1], j[2])
            if key not in seen:
                seen[key] = len(wgroups)
                wgroups.append(key)
        jgroup = []
        for j in jobs:
            key = ("Wg", j[1]) if j[0] == "G" else ("E", j[1], j[2])
            jgroup.append(seen[key])

        wready = [None] * len(wgroups)
        wb_done = [None] * len(wgroups)
        glast_mm = [None] * len(wgroups)
        lhsT_done = [None] * NTG
        xn_done = [None] * NTG
        sqpre_done = [None] * NTG
        zev_done = [None] * NTG
        mixevac = [None] * NTG
        selstt = [None] * NTG
        final_stt = [None] * NTG
        e7mm = [None] * NB
        out_st = [None] * NTG
        f11_done = [None] * NTG
        ln1_read = [None] * NB
        psfree = [None, None]
        tpY_done = [None] * NTG
        psT_free = None
        psB_free = None
        psM_free = tp3e
        xq_free = [None] * 4
        cast_jobs = {"last_cast": wg_cast_done}

        def emit_prepass(bb):
            nonlocal psT_free, psB_free
            s1l = []
            for tgb in range(TGB):
                tg = bb * TGB + tgb
                T0 = tg * 128
                xq = xq_[tgb]
                xw = [pass2_done]
                if xq_free[tgb] is not None:
                    xw = xw + xq_free[tgb]
                xcl = SY(lambda e, xq=xq, T0=T0: e.dma_start(xq, xs[T0:T0 + 128, :]),
                         waits=xw, inc=(s_x0 if tg % 2 == 0 else s_x1))
                wy = [pass2_done]
                if psT_free is not None:
                    wy.append(psT_free)
                lastT = None
                for g in range(G):
                    lastT = P_(lambda e, g=g, T0=T0: e.transpose(
                        psT[:, g * 128:(g + 1) * 128],
                        NXb[:, g * TOK + T0: g * TOK + T0 + 128], id_f[:, :]),
                        waits=wy if g == 0 else [])
                tpY_done[tg] = lastT
                ra = V(lambda e, xq=xq, tgb=tgb: e.scalar_tensor_tensor(
                    xq[:, 0:512], psT[:, 0:512], 1.0, xq[:, 0:512],
                    ALU.mult, ALU.add, accum_out=stQh[:, tgb * 2:tgb * 2 + 1]),
                    waits=[xcl, lastT])
                rb = V(lambda e, xq=xq, tgb=tgb: e.scalar_tensor_tensor(
                    xq[:, 512:1024], psT[:, 512:1024], 1.0, xq[:, 512:1024],
                    ALU.mult, ALU.add,
                    accum_out=stQh[:, tgb * 2 + 1:tgb * 2 + 2]),
                    waits=[xcl, lastT, ra])
                psT_free = rb
                s1l.append(V(lambda e, tg=tg, tgb=tgb: e.tensor_tensor(
                    stQ[:, tg:tg + 1], stQh[:, tgb * 2:tgb * 2 + 1],
                    stQh[:, tgb * 2 + 1:tgb * 2 + 2], ALU.add),
                    waits=[ra, rb]))
                zq = ZS_[bb % 2][:, tgb * D:(tgb + 1) * D]
                wsq = [ra, rb]
                if bb >= 2:
                    wsq.append(final_stt[(bb - 2) * TGB + tgb])
                    wsq.append(out_st[(bb - 2) * TGB + tgb])
                sq = V(lambda e, xq=xq, zq=zq, tg=tg: e.scalar_tensor_tensor(
                    zq, xq, 1.0, xq, ALU.mult, ALU.mult,
                    accum_out=stQ2[:, tg:tg + 1]), waits=wsq)
                sqpre_done[tg] = sq
                if STAGE == 41 and bb == 0 and tgb == 1:
                    pv = W.op("sync", lambda e: e.dma_start(
                        ys128[:, 0:1024], xq_[0]), [s1l[0], sqpre_done[0], sq,
                                                    s1l[1]], s_dbg, 16)
                    pv = W.op("sync", lambda e: e.dma_start(
                        ys128[:, 1024:2048], xq_[1]), [pv], s_dbg, 16)
                    pv = W.op("sync", lambda e: e.dma_start(
                        ys128[:, 2048:2052], stQ[:, 0:4]), [pv], s_dbg, 16)
                    pv = W.op("sync", lambda e: e.dma_start(
                        ys128[:, 2052:2056], stQ2[:, 0:4]), [pv], s_dbg, 16)
                    pv = W.op("sync", lambda e: e.dma_start(
                        ys128[:, 2056:2060], stQh[:, 0:4]), [pv], s_dbg, 16)
                    W.op("sync", lambda e: e.wait_ge(s_dbg, 5 * 16), [], None)
                    return True
            b4 = slice(bb * TGB, bb * TGB + TGB)
            w_ch = s1l + [sqpre_done[bb * TGB + t] for t in range(TGB)]
            cm = V(lambda e, b4=b4: e.tensor_scalar_mul(
                stM[:, b4], stQ[:, b4], 1.0 / D), waits=w_ch)
            cv = V(lambda e, b4=b4: e.tensor_scalar(
                stQ[:, b4], stQ2[:, b4], 1.0 / D, 1e-5, ALU.mult, ALU.add),
                waits=[cm])
            cm2 = V(lambda e, b4=b4: e.tensor_tensor(
                stQ2[:, b4], stM[:, b4], stM[:, b4], ALU.mult), waits=[cv])
            cv2 = V(lambda e, b4=b4: e.tensor_tensor(
                stQ[:, b4], stQ[:, b4], stQ2[:, b4], ALU.subtract), waits=[cm2])
            cl = A_(lambda e, b4=b4: e.activation(
                stQ2[:, b4], stQ[:, b4], AF.Ln), waits=[cv2])
            cr = A_(lambda e, b4=b4: e.activation(
                stR[:, b4], stQ2[:, b4], AF.Exp, scale=-0.5), waits=[cl])
            ln1_read[bb] = cv2
            for tgb in range(TGB):
                tg = bb * TGB + tgb
                xq3t = xq3[tgb]
                wxn = [cr, sqpre_done[tg], tpY_done[tg]]
                xn = V(lambda e, xq3t=xq3t, tg=tg: e.tensor_scalar(
                    ACC(tg), xq3t, stM[:, tg:tg + 1], stR[:, tg:tg + 1],
                    ALU.subtract, ALU.mult), waits=wxn)
                xn_done[tg] = xn
                xq_free[tgb] = [xn, sqpre_done[tg]]
                wcb = [xn]
                if tg >= 2:
                    wcb.append(lhsT_done[tg - 2])
                cb = A_(lambda e, tg=tg: e.copy(
                    xnb3[:, tg % 2, :, :], ACC(tg)), waits=wcb)
                wlb = [cb, c_idb]
                if psB_free is not None:
                    wlb.append(psB_free)
                lastB = None
                for g in range(G):
                    lastB = P_(lambda e, g=g, tg=tg: e.transpose(
                        psB[:, g * 128:(g + 1) * 128],
                        xnb[:, (tg % 2) * D + g * 128:(tg % 2) * D + (g + 1) * 128],
                        id_b[:, :]), waits=wlb if g == 0 else [])
                wlbe = [lastB]
                if bb >= 2:
                    wlbe.append(e7mm[bb - 2])
                    wlbe.append(f11_done[(bb - 2) * TGB + 3])
                lb = V(lambda e, tg=tg, tgb=tgb, bb=bb: e.tensor_copy(
                    lhsT_[bb % 2][:, tgb * D:(tgb + 1) * D], psB[:, :]), waits=wlbe)
                psB_free = lb
                lhsT_done[tg] = lb

        def emit_softmax_mix(bb):
            nonlocal psM_free
            ZR = ZS_[bb % 2]
            Z3 = ZR.rearrange("p (t d) -> p t d", d=D)
            w1 = [zev_done[bb * TGB + t] for t in range(TGB)]
            x1 = V(lambda e, Z3=Z3: e.reduce_max(nmz4[:, :], Z3, axis=AX.X,
                                                 negate=True), waits=w1)
            x2 = V(lambda e: e.tensor_scalar_mul(mzt4[:, :], nmz4[:, :], -1.0),
                   waits=[x1])
            x3l = [None] * TGB
            x4l = [None] * TGB
            for t in range(TGB):
                x3l[t] = V(lambda e, t=t, ZR=ZR: e.max_index(
                    idx4[:, t * 8:(t + 1) * 8],
                    mzt4[:, t:t + 1].broadcast_to([128, 8]),
                    ZR[:, t * D:(t + 1) * D]), waits=[x2] if t == 0 else [x2, x3l[t - 1]])
                x4l[t] = A_(lambda e, t=t, ZR=ZR: e.activation(
                    ZR[:, t * D:(t + 1) * D], ZR[:, t * D:(t + 1) * D], AF.Exp,
                    bias=nmz4[:, t:t + 1]), waits=[x3l[t]])
            x4b = V(lambda e, Z3=Z3: e.reduce_sum(ssum4[:, :], Z3, axis=AX.X),
                    waits=[x4l[TGB - 1]] + x4l[:TGB - 1])
            x5 = V(lambda e: e.reciprocal(p4[:, :], ssum4[:, :]), waits=[x4b])
            idv = idx4.ap().rearrange("p (t e) -> p t e", e=8)
            e8v = e8u4.ap().rearrange("p (t o) -> p t o", o=1)
            x7 = V(lambda e, idv=idv, e8v=e8v: e.tensor_scalar(
                e8v, idv[:, :, 0:1], 7, None, ALU.bitwise_and),
                waits=[x3l[TGB - 1]] + x3l[:TGB - 1])
            x8 = V(lambda e: e.tensor_copy(e8f4[:, :], e8u4[:, :]), waits=[x7])
            for t in range(TGB):
                tg = bb * TGB + t
                m1 = V(lambda e, t=t: e.tensor_tensor(
                    me8[:, :], e8f4[:, t:t + 1].broadcast_to([128, 8]), iota[:, :],
                    ALU.is_equal), waits=[x8, x5, ld_all] if t == 0 else [psM_free])
                m2 = V(lambda e, t=t, bb=bb: e.tensor_scalar(
                    alpha[:, (bb % 2) * 32 + t * 8:(bb % 2) * 32 + (t + 1) * 8],
                    me8[:, :], p4[:, t:t + 1], None, ALU.mult), waits=[m1])
                tpA = P_(lambda e: e.transpose(
                    psM[0:8, 0:128], me8[:, :], id_f[:, :]), waits=[m1, psM_free])
                evA = V(lambda e, t=t: e.tensor_copy(
                    alphaT[0:8, t * 128:(t + 1) * 128], psM[0:8, 0:128]),
                    waits=[tpA, m2])
                mixe = None
                for hf in range(2):
                    mxa = P_(lambda e, t=t, hf=hf: e.matmul(
                        psM[:, :], alphaT[0:8, t * 128:(t + 1) * 128],
                        behi16[0:8, hf * 512:(hf + 1) * 512], start=True, stop=False),
                        waits=[evA, be_ready] if hf == 0 else [mixe])
                    mx = P_(lambda e, t=t, hf=hf: e.matmul(
                        psM[:, :], alphaT[0:8, t * 128:(t + 1) * 128],
                        belo16[0:8, hf * 512:(hf + 1) * 512], start=False, stop=True),
                        waits=[])
                    wmx = [mx, x4b, x5]
                    if bb >= 2:
                        wmx.append(final_stt[(bb - 2) * TGB + t])
                    mixe = A_(lambda e, t=t, hf=hf, bb=bb: e.activation(
                        ZS_[bb % 2][:, t * D + hf * 512: t * D + (hf + 1) * 512],
                        psM[:, :], AF.Identity, scale=p4[:, t:t + 1]), waits=wmx)
                psM_free = mixe
                mixevac[tg] = mixe
            alpha_last = mixevac[bb * TGB + TGB - 1]

        def emit_ln2(b):
            b4 = slice(b * TGB, b * TGB + TGB)
            for tgb in range(TGB):
                tg = b * TGB + tgb
                zq3 = ZS_[b % 2][:, tgb * D:(tgb + 1) * D].rearrange(
                    "p (g t) -> p g t", t=128)
                fs = V(lambda e, zq3=zq3, tg=tg: e.scalar_tensor_tensor(
                    ACC(tg), zq3, 0.0, ACC(tg), ALU.max, ALU.add,
                    accum_out=stQ[:, tg:tg + 1]), waits=[selstt[tg]])
                final_stt[tg] = fs
                scr3 = scrapL[b % 2][:, (tgb % 2) * D:(tgb % 2 + 1) * D].rearrange(
                    "p (g t) -> p g t", t=128)
                sq = V(lambda e, scr3=scr3, tg=tg: e.scalar_tensor_tensor(
                    scr3, ACC(tg), 1.0, ACC(tg), ALU.mult, ALU.mult,
                    accum_out=stQ2[:, tg:tg + 1]), waits=[fs, e7mm[b]])
                final_stt[tg] = sq
            w_ch = [final_stt[b * TGB + t] for t in range(TGB)]
            cm = V(lambda e, b4=b4: e.tensor_scalar_mul(
                stM[:, b4], stQ[:, b4], 1.0 / D), waits=w_ch)
            cv = V(lambda e, b4=b4: e.tensor_scalar(
                stQ[:, b4], stQ2[:, b4], 1.0 / D, 1e-5, ALU.mult, ALU.add),
                waits=[cm])
            cm2 = V(lambda e, b4=b4: e.tensor_tensor(
                stQ2[:, b4], stM[:, b4], stM[:, b4], ALU.mult), waits=[cv])
            cv2 = V(lambda e, b4=b4: e.tensor_tensor(
                stQ[:, b4], stQ[:, b4], stQ2[:, b4], ALU.subtract), waits=[cm2])
            cl = A_(lambda e, b4=b4: e.activation(
                stQ2[:, b4], stQ[:, b4], AF.Ln), waits=[cv2])
            cr = A_(lambda e, b4=b4: e.activation(
                stR[:, b4], stQ2[:, b4], AF.Exp, scale=-0.5), waits=[cl])
            c7 = V(lambda e, b4=b4: e.tensor_tensor(
                stQ[:, b4], stM[:, b4], stR[:, b4], ALU.mult), waits=[cr])
            c8 = V(lambda e, b4=b4: e.tensor_scalar_mul(
                stQ[:, b4], stQ[:, b4], -1.0), waits=[c7])
            for tgb in range(TGB):
                tg = b * TGB + tgb
                zq = ZS_[b % 2][:, tgb * D:(tgb + 1) * D]
                zq3 = zq.rearrange("p (g t) -> p g t", t=128)
                f11 = A_(lambda e, tg=tg, zq3=zq3: e.activation(
                    zq3, ACC(tg), AF.Identity,
                    scale=stR[:, tg:tg + 1], bias=stQ[:, tg:tg + 1]),
                    waits=[c8, final_stt[tg]])
                f11_done[tg] = f11
                out_st[tg] = SY(lambda e, tg=tg, zq=zq: e.dma_start(
                    ys[tg * 128:(tg + 1) * 128, :], zq[:, :]), waits=[f11],
                    inc=(s_o0 if tg % 2 == 0 else s_o1))

        def emit_wload(gi):
            if wready[gi] is not None:
                return
            key = wgroups[gi]
            slot = We_[gi % 2]
            base_w = []
            if gi >= 2:
                base_w.append(glast_mm[gi - 2])
                if wb_done[gi - 2] is not None:
                    base_w.append(wb_done[gi - 2])
            if key[0] == "Wg":
                if key[1] == 0:
                    wready[gi] = [wg_cast_done]
                    return
                h0 = GW(lambda e, slot=slot: e.dma_start(
                    slot[:, 0:4 * D], wg16[:, 0:4 * D]),
                    waits=base_w + [wg_wb], inc=(s_we0 if gi % 2 == 0 else s_we1))
                h1 = SY(lambda e, slot=slot: e.dma_start(
                    slot[:, 4 * D:8 * D], wg16[:, 4 * D:8 * D]),
                    waits=base_w + [wg_wb], inc=s_w1)
                wready[gi] = [h0, h1]
            elif key[1] == 0:
                ei = key[2]
                prevc = [None, None]
                lastc = None
                for g in range(G):
                    if prevc[g % 2] is not None:
                        wd = [prevc[g % 2]]
                    else:
                        wd = list(base_w) + [wg_wb, cast_jobs["last_cast"], tp_last[3]]
                    if g % 2 == 0:
                        dd = GW(lambda e, ei=ei, g=g: e.dma_start(
                            stg_[g % 2], Wed[ei, g * 128:(g + 1) * 128, :]), waits=wd,
                            inc=s_w0)
                        lastc = A_(lambda e, g=g, slot=slot: e.copy(
                            slot[:, g * D:(g + 1) * D], stg_[g % 2]), waits=[dd])
                    else:
                        dd = SY(lambda e, ei=ei, g=g: e.dma_start(
                            stg_[g % 2], Wed[ei, g * 128:(g + 1) * 128, :]), waits=wd,
                            inc=s_w1)
                        lastc = V(lambda e, g=g, slot=slot: e.tensor_copy(
                            slot[:, g * D:(g + 1) * D], stg_[g % 2]), waits=[dd])
                    prevc[g % 2] = lastc
                cast_jobs["last_cast"] = lastc
                wready[gi] = [lastc]
                wb_done[gi] = SY(lambda e, ei=ei, slot=slot: e.dma_start(
                    we16[ei, :, :], slot[:, :]), waits=[prevc[0], prevc[1]], inc=s_wb)
            else:
                ei = key[2]
                base_w.append(wb_done[seen[("E", 0, ei)]])
                h0 = GW(lambda e, slot=slot, ei=ei: e.dma_start(
                    slot[:, 0:4 * D], we16[ei, :, 0:4 * D]), waits=base_w,
                    inc=(s_we0 if gi % 2 == 0 else s_we1))
                h1 = SY(lambda e, slot=slot, ei=ei: e.dma_start(
                    slot[:, 4 * D:8 * D], we16[ei, :, 4 * D:8 * D]), waits=base_w,
                    inc=s_w1)
                wready[gi] = [h0, h1]

        emit_wload(0)
        if emit_prepass(0):
            return
        emit_wload(1)

        if STAGE == 42:
            pv = W.op("sync", lambda e: e.dma_start(ys128[:, 0:12288],
                      NXb.ap()[:, 0:12288]),
                      [xn_done[t] for t in range(4)]
                      + [lhsT_done[t] for t in range(4)], s_dbg, 16)
            pv = W.op("sync", lambda e: e.dma_start(
                ys128[:, 12288:12292], stM[:, 0:4]), [pv], s_dbg, 16)
            pv = W.op("sync", lambda e: e.dma_start(
                ys128[:, 12292:12296], stR[:, 0:4]), [pv], s_dbg, 16)
            pv = W.op("sync", lambda e: e.dma_start(
                ys128[:, 12296:12300], stQ[:, 0:4]), [pv], s_dbg, 16)
            pv = W.op("sync", lambda e: e.dma_start(
                ys128[:, 12300:12304], stQ2[:, 0:4]), [pv], s_dbg, 16)
            W.op("sync", lambda e: e.wait_ge(s_dbg, 5 * 16), [], None)
            return

        for ji, job in enumerate(jobs):
            gi = jgroup[ji]
            tile = PS[ji % 2]
            if job[0] == "G":
                bb, tgb = job[1], job[2]
                tg = bb * TGB + tgb
                wm = list(wready[gi]) + [lhsT_done[tg]]
                if psfree[ji % 2] is not None:
                    wm.append(psfree[ji % 2])
                lastmm = None
                for g in range(G):
                    for bank in range(2):
                        lastmm = P_(lambda e, tile=tile, g=g, bank=bank, tgb=tgb, bb=bb:
                                    e.matmul(
                            tile[:, bank * 512:(bank + 1) * 512],
                            lhsT_[bb % 2][:, tgb * D + g * 128: tgb * D + (g + 1) * 128],
                            We_[gi % 2][:, g * D + bank * 512: g * D + bank * 512 + 512],
                            start=(g == 0), stop=(g == G - 1)),
                            waits=wm if (g == 0 and bank == 0) else [])
                glast_mm[gi] = lastmm
                zq = ZS_[bb % 2][:, tgb * D:(tgb + 1) * D]
                wz = [lastmm, bg_ready, sqpre_done[tg]]
                if bb >= 2:
                    wz.append(final_stt[(bb - 2) * TGB + tgb])
                zev_done[tg] = V(lambda e, zq=zq, tile=tile: e.scalar_tensor_tensor(
                    zq, tile[:, :], 1.0, BGfull[:, :], ALU.mult, ALU.add), waits=wz)
                psfree[ji % 2] = zev_done[tg]
                if tgb == TGB - 1 and bb == 0:
                    if STAGE == 43:
                        W.op("sync", lambda e: e.dma_start(
                            ys128[:, 0:4096], ZS_[0][:, :]),
                            [zev_done[t] for t in range(4)], s_dbg, 16)
                        W.op("sync", lambda e: e.wait_ge(s_dbg, 16), [], None)
                        return
                    emit_softmax_mix(bb)
                    if STAGE == 44:
                        pv = W.op("sync", lambda e: e.dma_start(
                            ys128[:, 0:4096], ZS_[0][:, :]),
                            [mixevac[t] for t in range(4)], s_dbg, 16)
                        pv = W.op("sync", lambda e: e.dma_start(
                            ys128[:, 4096:4128], alpha[:, 0:32]), [pv], s_dbg, 16)
                        pv = W.op("sync", lambda e: e.dma_start(
                            ys128[:, 4128:4132], p4[:, :]), [pv], s_dbg, 16)
                        pv = W.op("sync", lambda e: e.dma_start(
                            ys128[:, 4132:4136], e8f4[:, :]), [pv], s_dbg, 16)
                        W.op("sync", lambda e: e.wait_ge(s_dbg, 64), [], None)
                        return
            else:
                b, ei, tgb = job[1], job[2], job[3]
                tg = b * TGB + tgb
                wm = list(wready[gi]) + [lhsT_done[tg]]
                if psfree[ji % 2] is not None:
                    wm.append(psfree[ji % 2])
                lastmm = None
                for g in range(G):
                    for bank in range(2):
                        lastmm = P_(lambda e, tile=tile, g=g, bank=bank, tgb=tgb, b=b, gi=gi:
                                    e.matmul(
                            tile[:, bank * 512:(bank + 1) * 512],
                            lhsT_[b % 2][:, tgb * D + g * 128: tgb * D + (g + 1) * 128],
                            We_[gi % 2][:, g * D + bank * 512: g * D + bank * 512 + 512],
                            start=(g == 0), stop=(g == G - 1)),
                            waits=wm if (g == 0 and bank == 0) else [])
                glast_mm[gi] = lastmm
                if ei == G - 1 and tgb == TGB - 1:
                    e7mm[b] = lastmm
                zq = ZS_[b % 2][:, tgb * D:(tgb + 1) * D]
                prev_acc = mixevac[tg] if ei == 0 else selstt[tg]
                ss = V(lambda e, zq=zq, tile=tile, b=b, tg=tg, tgb=tgb, ei=ei:
                       e.scalar_tensor_tensor(
                    zq, tile[:, :],
                    alpha[:, (b % 2) * 32 + tgb * 8 + ei:(b % 2) * 32 + tgb * 8 + ei + 1],
                    zq, ALU.mult, ALU.add), waits=[lastmm, prev_acc])
                selstt[tg] = ss
                psfree[ji % 2] = ss
                if ei == 1 and tgb == TGB - 1 and b + 1 < NB:
                    emit_prepass(b + 1)
                if ei == 4 and tgb == TGB - 1 and b + 1 < NB:
                    emit_softmax_mix(b + 1)
                if ei == G - 1 and tgb == TGB - 1:
                    emit_ln2(b)
            if ji + 1 < len(jobs) and jgroup[ji + 1] != gi:
                nxt = gi + 2
                if nxt < len(wgroups):
                    emit_wload(nxt)

        SY(lambda e: e.wait_ge(s_o0, 16 * (NTG // 2)), waits=[], inc=None)
        SY(lambda e: e.wait_ge(s_o1, 16 * (NTG // 2)), waits=[], inc=None)

    cnt = Walk("count")
    schedule(cnt)

    with nc.Block() as block:
        for key, starter in [("sync", block.sync), ("vector", block.vector),
                             ("scalar", block.scalar), ("tensor", block.tensor),
                             ("gpsimd", block.gpsimd)]:
            def body(eng, key=key):
                w = Walk(key, cnt.events)
                w.eng = eng
                schedule(w)
            starter(body)

    stack.close()
    return nc


_NC_CACHE = {}


def kernel(L0_attn_x, L0_attn_a_param, L0_attn_w_in, L0_attn_b_in, L0_attn_w_a,
           L0_attn_b_a, L0_ffn_Wg, L0_ffn_bg, L0_ffn_We, L0_ffn_be):
    f32 = np.float32
    x = np.ascontiguousarray(np.asarray(L0_attn_x, dtype=f32))
    B, L, Dm = x.shape
    shared = dict(
        win=np.ascontiguousarray(np.asarray(L0_attn_w_in, f32)),
        wa=np.ascontiguousarray(np.asarray(L0_attn_w_a, f32)),
        bin_t=np.ascontiguousarray(np.asarray(L0_attn_b_in, f32).T),
        ba_t=np.ascontiguousarray(np.asarray(L0_attn_b_a, f32).T),
        apar_t=np.ascontiguousarray(np.asarray(L0_attn_a_param, f32).reshape(G, 128).T),
        Wgd=np.ascontiguousarray(np.asarray(L0_ffn_Wg, f32)),
        Wed=np.ascontiguousarray(np.asarray(L0_ffn_We, f32)),
        bgd=np.ascontiguousarray(np.asarray(L0_ffn_bg, f32).reshape(1, Dm)),
        bed=np.ascontiguousarray(np.asarray(L0_ffn_be, f32)),
        ident=np.eye(128, dtype=f32),
        iot8=np.ascontiguousarray(np.tile(np.arange(8, dtype=f32), (128, 1))),
    )
    in_maps = []
    for c in range(8):
        b, k = c // 4, c % 4
        m = dict(shared)
        m["xs"] = np.ascontiguousarray(x[b, k * TOK:(k + 1) * TOK, :])
        ms = np.zeros((8, 8), f32)
        for j in range(8):
            if j // 4 == b and j % 4 < k:
                ms[:, j] = 1.0
        m["msel"] = ms
        in_maps.append(m)

    if "nc" not in _NC_CACHE:
        _NC_CACHE["nc"] = build_program()
    nc = _NC_CACHE["nc"]
    res = run_bass_kernel_spmd(nc, in_maps, core_ids=list(range(8)))
    out = np.empty((B, L, Dm), dtype=np.float32)
    for c in range(8):
        b, k = c // 4, c % 4
        out[b, k * TOK:(k + 1) * TOK, :] = res.results[c]["ys"]
    return out


# version 11
# speedup vs baseline: 1.1132x; 1.1132x over previous
import os
import numpy as np
import concourse.bass as bass
import concourse.mybir as mybir
from concourse.bass_utils import run_bass_kernel_spmd

F32 = mybir.dt.float32
BF16 = mybir.dt.bfloat16
U32 = mybir.dt.uint32
AF = mybir.ActivationFunctionType
ALU = mybir.AluOpType
AX = mybir.AxisListType

D = 1024
G = 8
TOK = 2048
NTILE = 16
NTG = 16
NB = 4
TGB = 4
HALF = 1024


def _skey(sem):
    return getattr(sem, "name", None) or id(sem)


class Walk:

    def __init__(self, mode, events=None):
        self.mode = mode
        self.events = events if events is not None else []
        self.n = 0
        self.counts = {}
        self.waited = {}
        self.eng = None

    def op(self, eng_key, emit_fn, waits=(), inc=None, inc_n=1):
        ev = None
        if inc is not None:
            ev = self.n
            self.n += 1
        if self.mode == "count":
            if inc is not None:
                k = _skey(inc)
                self.counts[k] = self.counts.get(k, 0) + inc_n
                self.events.append((inc, self.counts[k]))
        elif eng_key == self.mode:
            for w in waits:
                if w is None:
                    continue
                sem, val = self.events[w]
                k = _skey(sem)
                if self.waited.get(k, 0) < val:
                    self.eng.wait_ge(sem, val)
                    self.waited[k] = val
            instr = emit_fn(self.eng)
            if inc is not None and instr is not None:
                instr.then_inc(inc, inc_n)
        return ev


def build_program():
    nc = bass.Bass("TRN2", target_bir_lowering=False)

    def inp(name, shape, dt=F32):
        return nc.dram_tensor(name, shape, dt, kind="ExternalInput").ap()

    xs = inp("xs", [TOK, D])
    win = inp("win", [G, 128, 128])
    wa = inp("wa", [G, 128, 128])
    bin_t = inp("bin_t", [128, G])
    ba_t = inp("ba_t", [128, G])
    apar_t = inp("apar_t", [128, G])
    Wgd = inp("Wgd", [D, D])
    Wed = inp("Wed", [G, D, D])
    bgd = inp("bgd", [1, D])
    bed = inp("bed", [G, D])
    ident = inp("ident", [128, 128])
    msel = inp("msel", [8, 8])
    iot8 = inp("iot8", [128, 8])
    ys = nc.dram_tensor("ys", [TOK, D], F32, kind="ExternalOutput").ap()

    cc_in = nc.dram_tensor("cc_in", [2, D], F32).ap()
    cc_out = nc.dram_tensor("cc_out", [16, D], F32).ap()
    we16 = nc.dram_tensor("we16", [G, 128, G * D], BF16).ap()
    wg16 = nc.dram_tensor("wg16", [128, G * D], BF16).ap()

    from contextlib import ExitStack
    stack = ExitStack()
    sb = lambda n, s, d=F32: stack.enter_context(nc.sbuf_tensor(n, s, d))

    UN = sb("UN", [128, 2048])
    NXb = sb("NXb", [128, G * TOK])
    ABIG = sb("ABIG", [128, G * TOK])
    Rb = sb("Rb", [128, 45056], mybir.dt.uint8)
    id_f = sb("id_f", [128, 128])
    id_b = sb("id_b", [128, 128], BF16)
    win_b = sb("win_b", [128, G * 128], BF16)
    wa_b = sb("wa_b", [128, G * 128], BF16)
    binh = sb("binh", [128, G])
    bah = sb("bah", [128, G])
    apar = sb("apar", [128, G])
    scl = sb("scl", [128, G])
    sclh = sb("sclh", [128, G])
    sclb = sb("sclb", [128, G])
    spt = sb("spt", [128, G])
    spt2 = sb("spt2", [128, G])
    gaS = sb("gaS", [128, 16])
    gaQ = sb("gaQ", [128, 32])
    msl1m = sb("msl1m", [8, 8])
    Pha = sb("Pha", [128, 16])
    Fha = sb("Fha", [128, 16])
    PcFc = sb("PcFc", [128, 16])
    ccsb = sb("ccsb", [8, 256])
    msl = sb("msl", [8, 8])
    Htile = sb("Htile", [8, 128])
    Ht = sb("Ht", [128, 8])
    Gt = sb("Gt", [128, 16])
    t8a = sb("t8a", [8, 128])
    t8b = sb("t8b", [8, 128])
    iota = sb("iota", [128, 8])
    bgf = sb("bgf", [1, D])
    BGfull = sb("BGfull", [128, D])
    ones1 = sb("ones1", [1, 128])
    alphaT = sb("alphaT", [8, TGB * 128], BF16)
    behi16 = sb("behi16", [8, D], BF16)
    belo16 = sb("belo16", [8, D], BF16)
    alpha = sb("alpha", [128, 2 * 32])
    idx4 = sb("idx4", [128, 32], U32)
    e8u4 = sb("e8u4", [128, 4], U32)
    e8f4 = sb("e8f4", [128, 4])
    me8 = sb("me8", [128, 8])
    nmz4 = sb("nmz4", [128, 4])
    mzt4 = sb("mzt4", [128, 4])
    ssum4 = sb("ssum4", [128, 4])
    p4 = sb("p4", [128, 4])
    stM = sb("stM", [128, NTG])
    stR = sb("stR", [128, NTG])
    stQ = sb("stQ", [128, NTG])
    stQ2 = sb("stQ2", [128, NTG])
    stQh = sb("stQh", [128, 8])
    xnb = sb("xnb", [128, 2 * D], BF16)

    rb8 = Rb.ap()

    def rview(off_b, p, fsz, dt):
        return rb8[0:p, off_b:off_b + fsz * mybir.dt.size(dt)].bitcast(dt)

    xA4 = [rview(0, 128, 4 * D, F32), rview(16384, 128, 4 * D, F32)]
    xbg_ = [rview(32768, 128, 512, BF16), rview(33792, 128, 512, BF16)]
    xf_ = [rview(34816 + 2048 * i, 128, 512, F32) for i in range(3)]
    winf = rview(32768, 128, G * 128, F32)
    waf = rview(36864, 128, G * 128, F32)
    We_ = [rview(0, 128, G * D, BF16), rview(16384, 128, G * D, BF16)]
    stg_ = [rview(32768, 128, D, F32), rview(36864, 128, D, F32)]
    beds = rview(40960, 8, D, F32)

    ab8 = ABIG.ap().bitcast(mybir.dt.uint8)

    def aview(off_b, p, fsz, dt):
        return ab8[0:p, off_b:off_b + fsz * mybir.dt.size(dt)].bitcast(dt)

    lhsT_ = [aview(0, 128, 4 * D, BF16), aview(8192, 128, 4 * D, BF16)]
    ZS_ = [aview(16384, 128, 4 * D, F32), aview(32768, 128, 4 * D, F32)]
    xq_ = [aview(49152 + 4096 * i, 128, D, F32) for i in range(4)]
    scrapL = [aview(0, 128, 2 * D, F32), aview(8192, 128, 2 * D, F32)]

    psT = stack.enter_context(nc.psum_tensor("psT", [128, 1024], F32))
    psB = stack.enter_context(nc.psum_tensor("psB", [128, 1024], BF16))
    psM = stack.enter_context(nc.psum_tensor("psM", [128, 512], F32))
    PS = [stack.enter_context(nc.psum_tensor("PS0", [128, 1024], F32)),
          stack.enter_context(nc.psum_tensor("PS1", [128, 1024], F32))]

    s_ld = stack.enter_context(nc.semaphore("s_ld"))
    s_x0 = stack.enter_context(nc.semaphore("s_x0"))
    s_x1 = stack.enter_context(nc.semaphore("s_x1"))
    s_w0 = stack.enter_context(nc.semaphore("s_w0"))
    s_w1 = stack.enter_context(nc.semaphore("s_w1"))
    s_we0 = stack.enter_context(nc.semaphore("s_we0"))
    s_we1 = stack.enter_context(nc.semaphore("s_we1"))
    s_wb = stack.enter_context(nc.semaphore("s_wb"))
    s_o0 = stack.enter_context(nc.semaphore("s_o0"))
    s_o1 = stack.enter_context(nc.semaphore("s_o1"))
    s_cd = stack.enter_context(nc.semaphore("s_cd"))
    s_dbg = stack.enter_context(nc.semaphore("s_dbg"))
    s_cc = stack.enter_context(nc.semaphore("s_cc"))
    s_v = stack.enter_context(nc.semaphore("s_v"))
    s_a = stack.enter_context(nc.semaphore("s_a"))
    s_p = stack.enter_context(nc.semaphore("s_p"))
    s_g = stack.enter_context(nc.semaphore("s_g"))

    PFg = UN.ap()[0:8, :]
    gxb = [UN.ap()[:, 0:512], UN.ap()[:, 512:1024]]
    gab = [UN.ap()[:, 1024:1536], UN.ap()[:, 1536:2048]]
    xnbf = xnb.ap().bitcast(F32)
    gaQv = gaQ.ap().rearrange("p (g c) -> p g c", c=4)
    NX3 = NXb.ap().rearrange("p (g t) -> p g t", t=TOK)
    AB3 = ABIG.ap().rearrange("p (g t) -> p g t", t=TOK)
    gaS3 = gaS.ap().rearrange("p (g h) -> p g h", h=2)
    Pv = Pha.ap().rearrange("p (g h) -> p h g", h=2)
    Fv = Fha.ap().rearrange("p (g h) -> p h g", h=2)
    Gv = Gt.ap().rearrange("p (g h) -> p h g", h=2)
    Phav = Pha.ap().rearrange("p (g h) -> p g h", h=2)

    def ACC(tg):
        return NX3[:, :, tg * 128:(tg + 1) * 128]

    STAGE = int(os.environ.get("KSTAGE", "9"))
    ys128 = ys.rearrange("(p r) c -> p (r c)", p=128)

    def schedule(W):
        V = lambda f, waits=(), inc=s_v, n=1: W.op("vector", f, waits, inc, n)
        A_ = lambda f, waits=(), inc=s_a, n=1: W.op("scalar", f, waits, inc, n)
        P_ = lambda f, waits=(), inc=s_p, n=1: W.op("tensor", f, waits, inc, n)
        G_ = lambda f, waits=(), inc=s_g, n=1: W.op("gpsimd", f, waits, inc, n)
        GW = lambda f, waits=(), inc=None, n=16: W.op("gpsimd", f, waits, inc, n)
        SY = lambda f, waits=(), inc=None, n=16: W.op("sync", f, waits, inc, n)

        SY(lambda e: e.dma_start(id_f[:, :], ident[:, :]), inc=s_ld)
        SY(lambda e: e.dma_start(
            winf.rearrange("i (g o) -> i g o", o=128),
            win.rearrange("g i o -> i g o")), inc=s_ld)
        SY(lambda e: e.dma_start(
            waf.rearrange("i (g o) -> i g o", o=128),
            wa.rearrange("g i o -> i g o")), inc=s_ld)
        SY(lambda e: e.dma_start(binh[:, :], bin_t[:, :]), inc=s_ld)
        SY(lambda e: e.dma_start(bah[:, :], ba_t[:, :]), inc=s_ld)
        SY(lambda e: e.dma_start(apar[:, :], apar_t[:, :]), inc=s_ld)
        SY(lambda e: e.dma_start(msl[:, :], msel[:, :]), inc=s_ld)
        SY(lambda e: e.dma_start(iota[:, :], iot8[:, :]), inc=s_ld)
        SY(lambda e: e.dma_start(bgf[:, :], bgd[:, :]), inc=s_ld)
        ld_all = SY(lambda e: e.dma_start(beds, bed[:, :]), inc=s_ld)

        c_idb = A_(lambda e: e.copy(id_b[:, :], id_f[:, :]), waits=[ld_all])
        hb1 = V(lambda e: e.tensor_scalar_mul(binh[:, :], binh[:, :], 0.5), waits=[ld_all])
        hb2 = V(lambda e: e.tensor_scalar_mul(bah[:, :], bah[:, :], 0.5), waits=[ld_all])
        c_win = A_(lambda e: e.copy(win_b[:, :], winf), waits=[ld_all])
        c_wa = A_(lambda e: e.copy(wa_b[:, :], waf), waits=[ld_all])
        sp1 = A_(lambda e: e.activation(spt[:, :], apar[:, :], AF.Abs), waits=[ld_all])
        sp2 = A_(lambda e: e.activation(spt[:, :], spt[:, :], AF.Exp, scale=-1.0), waits=[sp1])
        sp3 = A_(lambda e: e.activation(spt[:, :], spt[:, :], AF.Ln, bias=1.0), waits=[sp2])
        sp4 = A_(lambda e: e.activation(scl[:, :], apar[:, :], AF.Relu), waits=[ld_all])
        sp5 = V(lambda e: e.tensor_tensor(scl[:, :], scl[:, :], spt[:, :], ALU.add),
                waits=[sp3, sp4])
        sp6 = V(lambda e: e.tensor_scalar_mul(scl[:, :], scl[:, :], -8.0), waits=[sp5])
        sp7 = V(lambda e: e.tensor_scalar_mul(sclh[:, :], scl[:, :], 0.5), waits=[sp6])
        sp8 = V(lambda e: e.tensor_scalar_mul(sclb[:, :], scl[:, :], 512.0), waits=[sp6])
        z_gaS = V(lambda e: e.memset(gaS[:, :], 0.0))
        z_ones = V(lambda e: e.memset(ones1[:, :], 1.0))
        bh1 = A_(lambda e: e.copy(behi16[:, :], beds), waits=[ld_all])
        bh2 = V(lambda e: e.tensor_tensor(belo16[:, :], beds, behi16[:, :],
                                          ALU.subtract), waits=[bh1])
        be_ready = bh2
        bgm0 = P_(lambda e: e.matmul(psM[:, :], ones1[0:1, :], bgf[0:1, 0:512],
                                     start=True, stop=True), waits=[z_ones, ld_all])
        bge0 = V(lambda e: e.tensor_copy(BGfull[:, 0:512], psM[:, :]), waits=[bgm0])
        bgm1 = P_(lambda e: e.matmul(psM[:, :], ones1[0:1, :], bgf[0:1, 512:1024],
                                     start=True, stop=True), waits=[bge0])
        bge1 = V(lambda e: e.tensor_copy(BGfull[:, 512:1024], psM[:, :]), waits=[bgm1])
        bg_ready = bge1
        xq3 = [q.rearrange("p (g t) -> p g t", t=128) for q in xq_]
        xnb3 = xnb.ap().rearrange("p (s g t) -> p s g t", s=2, t=128)

        xld = [None] * 4
        tp_last = [None] * 4
        psT_c = [[None], [None]]
        xbg_free = [None, None]
        xf_free = [None, None, None]
        gx_free = {}
        ga_free = [None, None]
        ps0_free = [None, None]
        ps1_free = [None, None]
        aexv = {}
        nxfv = {}
        omv = {}
        sxv = {}
        stv = {}
        mma_last = None
        fha_prev = None
        gaSh = [None, None]
        phalf_l = None

        def seg(c, g):
            return NXb[:, g * TOK + c * 512: g * TOK + (c + 1) * 512]

        def aseg(c, g):
            return ABIG[:, g * TOK + c * 512: g * TOK + (c + 1) * 512]

        def emit_snx(u):
            fc, fg = u // G, u % G
            sxv[u] = A_(lambda e, fc=fc, fg=fg: e.activation(
                seg(fc, fg), seg(fc, fg), AF.Sqrt, scale=0.25), waits=[omv[u]])

        def emit_ttp(u):
            fc, fg = u // G, u % G
            tt = G_(lambda e, fc=fc, fg=fg, u=u: e.tensor_tensor(
                seg(fc, fg), seg(fc, fg), xf_[u % 3], ALU.mult),
                waits=[sxv[u], stv[u]])
            xf_free[u % 3] = tt
            nxfv[(fc, fg)] = tt
            return tt

        for c in range(4):
            xw = [tp_last[c - 2]] if c >= 2 else []
            xld[c] = SY(lambda e, c=c: e.dma_start(
                xA4[c % 2].rearrange("p (t f) -> p t f", f=D),
                xs[c * 512:(c + 1) * 512, :].rearrange("(t p) f -> p t f", p=128)),
                waits=xw, inc=(s_x0 if c % 2 == 0 else s_x1))
            if c == 2:
                gaSh[0] = V(lambda e: e.tensor_tensor(
                    gaS3[:, :, 0], gaQv[:, :, 0], gaQv[:, :, 1], ALU.add),
                    waits=[aexv[(1, G - 1)]])
            for g in range(G):
                u = c * G + g
                par = g % 2
                wtr = [xld[c], ld_all] + psT_c[par]
                lastT = None
                for i in range(4):
                    lastT = P_(lambda e, c=c, g=g, i=i, par=par: e.transpose(
                        psT[:, par * 512 + i * 128: par * 512 + (i + 1) * 128],
                        xA4[c % 2][:, i * D + g * 128: i * D + (g + 1) * 128],
                        id_f[:, :]), waits=wtr if i == 0 else [])
                if g == G - 1:
                    tp_last[c] = lastT
                web = [lastT]
                if xbg_free[par] is not None:
                    web.append(xbg_free[par])
                if c == 0 and g < 2:
                    web += [c_win, c_wa]
                ebf = V(lambda e, par=par: e.tensor_copy(
                    xbg_[par], psT[:, par * 512:(par + 1) * 512]), waits=web)
                we2 = [lastT]
                if xf_free[u % 3] is not None:
                    we2.append(xf_free[u % 3])
                if c == 0 and g < 3:
                    we2 += [c_win, c_wa]
                ebf2 = V(lambda e, u=u, par=par: e.tensor_copy(
                    xf_[u % 3], psT[:, par * 512:(par + 1) * 512]), waits=we2)
                psT_c[par] = [ebf, ebf2]
                wmx = [ebf, c_win]
                if ps0_free[par] is not None:
                    wmx.append(ps0_free[par])
                mmx = P_(lambda e, g=g, par=par: e.matmul(
                    PS[0][:, par * 512:(par + 1) * 512],
                    win_b[:, g * 128:(g + 1) * 128], xbg_[par],
                    start=True, stop=True), waits=wmx)
                wma = [c_wa]
                if ps1_free[par] is not None:
                    wma.append(ps1_free[par])
                mma = P_(lambda e, g=g, par=par: e.matmul(
                    PS[1][:, par * 512:(par + 1) * 512],
                    wa_b[:, g * 128:(g + 1) * 128], xbg_[par],
                    start=True, stop=True), waits=wma)
                xbg_free[par] = mma
                mma_last = mma
                wgx = [mmx, hb1]
                if u >= 2 and (u - 2) in stv:
                    wgx.append(stv[u - 2])
                tgx = A_(lambda e, g=g, par=par: e.activation(
                    gxb[par], PS[0][:, par * 512:(par + 1) * 512], AF.Tanh,
                    bias=binh[:, g:g + 1], scale=0.5), waits=wgx)
                ps0_free[par] = tgx
                wga = [mma, hb2]
                if ga_free[par] is not None:
                    wga.append(ga_free[par])
                tga = A_(lambda e, g=g, c=c, par=par: e.activation(
                    gab[par], PS[1][:, par * 512:(par + 1) * 512], AF.Tanh,
                    bias=bah[:, g:g + 1], scale=0.5,
                    accum_out=gaQ[:, g * 4 + c: g * 4 + c + 1]), waits=wga)
                ps1_free[par] = tga
                aex = A_(lambda e, g=g, c=c, par=par: e.activation(
                    aseg(c, g), gab[par], AF.Exp,
                    bias=sclh[:, g:g + 1], scale=sclh[:, g:g + 1]), waits=[tga, sp7])
                aexv[(c, g)] = aex
                ga_free[par] = aex
                stv[u] = V(lambda e, u=u, par=par: e.scalar_tensor_tensor(
                    xf_[u % 3], gxb[par], 1.0, xf_[u % 3], ALU.add, ALU.mult),
                    waits=[tgx, ebf2])
                t1 = G_(lambda e, c=c, g=g: e.tensor_tensor(
                    seg(c, g), aseg(c, g), aseg(c, g), ALU.mult), waits=[aex])
                omv[u] = G_(lambda e, c=c, g=g: e.tensor_scalar(
                    seg(c, g), seg(c, g), -1.0, 1.0, ALU.mult, ALU.add), waits=[t1])
                if u % 2 == 1:
                    emit_snx(u - 1)
                    emit_snx(u)
                if u >= 2:
                    emit_ttp(u - 2)
                if c == 2:
                    sc1 = V(lambda e, g=g: e.tensor_tensor_scan(
                        xnbf, ABIG[:, g * TOK: g * TOK + HALF],
                        NXb[:, g * TOK: g * TOK + HALF],
                        0.0, ALU.mult, ALU.add),
                        waits=[nxfv[(1, g)], aexv[(1, g)]]
                        + ([fha_prev] if fha_prev else []))
                    fha_prev = V(lambda e, g=g: e.tensor_copy(
                        Fha[:, g * 2: g * 2 + 1], xnbf[:, HALF - 1:HALF]),
                        waits=[sc1])
        emit_ttp(30)
        emit_ttp(31)
        stageA_done = [mma_last, nxfv[(3, G - 1)], tp_last[3], tp_last[2]]

        if STAGE == 1:
            dump_and_stop(NXb.ap(), stageA_done)
            return

        gaSh[1] = V(lambda e: e.tensor_tensor(
            gaS3[:, :, 1], gaQv[:, :, 2], gaQv[:, :, 3], ALU.add),
            waits=[aexv[(3, G - 1)]])
        for g in range(G):
            sc1 = V(lambda e, g=g: e.tensor_tensor_scan(
                xnbf, ABIG[:, g * TOK + HALF: g * TOK + 2 * HALF],
                NXb[:, g * TOK + HALF: g * TOK + 2 * HALF],
                0.0, ALU.mult, ALU.add),
                waits=[nxfv[(3, g)], aexv[(3, g)], fha_prev])
            fha_prev = V(lambda e, g=g: e.tensor_copy(
                Fha[:, g * 2 + 1: g * 2 + 2], xnbf[:, HALF - 1:HALF]), waits=[sc1])
            phalf_l = A_(lambda e, g=g: e.activation(
                Phav[:, g, :], gaS3[:, g, :], AF.Exp,
                bias=sclb[:, g:g + 1], scale=sclh[:, g:g + 1]),
                waits=[gaSh[1], gaSh[0], sp8] if g == 0 else [phalf_l])
        pass1_done = fha_prev
        phalf_done = phalf_l

        wg_c = [None] * G
        for g in range(G):
            wd = [mma_last, nxfv[(3, G - 1)]] if g < 2 else [wg_c[g - 2]]
            dd = GW(lambda e, g=g: e.dma_start(stg_[g % 2], Wgd[g * 128:(g + 1) * 128, :]),
                    waits=wd, inc=(s_w0 if g % 2 == 0 else s_w1))
            wc = [dd, tp_last[2]]
            wg_c[g] = A_(lambda e, g=g: e.copy(We_[0][:, g * D:(g + 1) * D], stg_[g % 2]),
                         waits=wc if g < 2 else [dd])
        wg_cast_done = wg_c[G - 1]
        wg_wb = SY(lambda e: e.dma_start(wg16[:, :], We_[0][:, :]), waits=[wg_cast_done],
                   inc=s_wb)

        c1 = V(lambda e: e.tensor_tensor(PcFc[:, 0:8], Pv[:, 1, :], Fv[:, 0, :], ALU.mult),
               waits=[pass1_done, phalf_done])
        c2 = V(lambda e: e.tensor_tensor(PcFc[:, 8:16], PcFc[:, 0:8], Fv[:, 1, :], ALU.add),
               waits=[c1])
        c3 = V(lambda e: e.tensor_tensor(PcFc[:, 0:8], Pv[:, 0, :], Pv[:, 1, :], ALU.mult),
               waits=[c2])
        tp1 = P_(lambda e: e.transpose(psM[0:8, 0:128], PcFc[:, 0:8], id_f[:, :]),
                 waits=[c3, bge1])
        tp1e = V(lambda e: e.tensor_copy(ccsb[:, 0:128], psM[0:8, 0:128]), waits=[tp1])
        tp2 = P_(lambda e: e.transpose(psM[0:8, 128:256], PcFc[:, 8:16], id_f[:, :]),
                 waits=[tp1e])
        tp2e = V(lambda e: e.tensor_copy(ccsb[:, 128:256], psM[0:8, 128:256]), waits=[tp2])
        d1 = G_(lambda e: e.dma_start(
            cc_in.rearrange("r (g p) -> g r p", p=128),
            ccsb[:, :].rearrange("g (r p) -> g r p", p=128)),
            waits=[tp1e, tp2e], inc=s_cd, n=16)
        cc = G_(lambda e: e.collective_compute(
            "AllGather", ALU.bypass, replica_groups=[list(range(8))],
            ins=[cc_in.opt()], outs=[cc_out.opt()]), waits=[d1], inc=s_cc, n=1)
        d2 = G_(lambda e: e.dma_start(
            PFg[:, :].rearrange("g (j p) -> g j p", p=128),
            cc_out.rearrange("j (g p) -> g j p", p=128)),
            waits=[cc, mma_last, nxfv[(3, G - 1)]], inc=s_cd, n=16)
        m1m = V(lambda e: e.tensor_scalar(msl1m[:, :], msl[:, :], -1.0, 1.0,
                                          ALU.mult, ALU.add), waits=[ld_all])
        bpf = []
        for j in range(8):
            bpf.append(V(lambda e, j=j: e.tensor_scalar(
                PFg[:, (2 * j) * 128:(2 * j + 1) * 128],
                PFg[:, (2 * j) * 128:(2 * j + 1) * 128],
                msl[:, j:j + 1], msl1m[:, j:j + 1], ALU.mult, ALU.add),
                waits=[d2, m1m]))
            bpf.append(V(lambda e, j=j: e.tensor_scalar(
                PFg[:, (2 * j + 1) * 128:(2 * j + 2) * 128],
                PFg[:, (2 * j + 1) * 128:(2 * j + 2) * 128],
                msl[:, j:j + 1], None, ALU.mult), waits=[d2]))
        prevh = V(lambda e: e.memset(Htile[:, :], 0.0))
        for j in range(8):
            q1 = V(lambda e, j=j: e.tensor_tensor(
                t8a[:, :], PFg[:, (2 * j) * 128:(2 * j + 1) * 128], Htile[:, :],
                ALU.mult), waits=[bpf[2 * j], prevh])
            prevh = V(lambda e, j=j: e.tensor_tensor(
                Htile[:, :], t8a[:, :], PFg[:, (2 * j + 1) * 128:(2 * j + 2) * 128],
                ALU.add), waits=[q1, bpf[2 * j + 1]])
        tp3 = P_(lambda e: e.transpose(psM[:, 256:264], Htile[:, :], id_f[0:8, 0:8]),
                 waits=[prevh, tp2e])
        tp3e = V(lambda e: e.tensor_copy(Ht[:, :], psM[:, 256:264]), waits=[tp3])
        g1 = V(lambda e: e.tensor_copy(Gv[:, 0, :], Ht[:, :]), waits=[tp3e])
        g2 = V(lambda e: e.tensor_tensor(Gv[:, 1, :], Pv[:, 0, :], Ht[:, :], ALU.mult),
               waits=[tp3e, g1])
        g3 = V(lambda e: e.tensor_tensor(Gv[:, 1, :], Gv[:, 1, :], Fv[:, 0, :], ALU.add),
               waits=[g2])

        pass2c = [None] * 4
        prev_cg = [None] * G
        for c in range(4):
            h = c // 2
            for g in range(G):
                base = g * TOK + c * 512
                if c % 2 == 0:
                    init = Gt[:, g * 2 + h: g * 2 + h + 1]
                    wts = [g1, g3] if (c == 0 and g == 0) else ([] if c == 0 else [prev_cg[g]])
                else:
                    init = NXb[:, base - 1: base]
                    wts = [prev_cg[g]]
                prev_cg[g] = V(lambda e, base=base, init=init: e.tensor_tensor_scan(
                    NXb[:, base: base + 512],
                    ABIG[:, base: base + 512],
                    NXb[:, base: base + 512],
                    init, ALU.mult, ALU.add), waits=wts)
            pass2c[c] = prev_cg[G - 1]
        pass2_done = pass2c[3]

        if STAGE == 3:
            dump_and_stop(NXb.ap(), [pass2_done])
            return

        jobs = []
        for t in range(TGB):
            jobs.append(("G", 0, t))
        for b in range(NB):
            for ei in range(3):
                for t in range(TGB):
                    jobs.append(("E", b, ei, t))
            if b < NB - 1:
                for t in range(TGB):
                    jobs.append(("G", b + 1, t))
            for ei in range(3, G):
                for t in range(TGB):
                    jobs.append(("E", b, ei, t))

        wgroups = []
        seen = {}
        for j in jobs:
            key = ("Wg", j[1]) if j[0] == "G" else ("E", j[1], j[2])
            if key not in seen:
                seen[key] = len(wgroups)
                wgroups.append(key)
        jgroup = []
        for j in jobs:
            key = ("Wg", j[1]) if j[0] == "G" else ("E", j[1], j[2])
            jgroup.append(seen[key])

        wready = [None] * len(wgroups)
        wb_done = [None] * len(wgroups)
        glast_mm = [None] * len(wgroups)
        lhsT_done = [None] * NTG
        xn_done = [None] * NTG
        sqpre_done = [None] * NTG
        zev_done = [None] * NTG
        mixevac = [None] * NTG
        selstt = [None] * NTG
        final_stt = [None] * NTG
        e7mm = [None] * NB
        out_st = [None] * NTG
        f11_done = [None] * NTG
        ln1_read = [None] * NB
        psfree = [None, None]
        tpY_done = [None] * NTG
        psT_free = None
        psB_free = None
        psM_free = tp3e
        xq_free = [None] * 4
        cast_jobs = {"last_cast": wg_cast_done}

        def emit_prepass(bb):
            nonlocal psT_free, psB_free
            s1l = []
            for tgb in range(TGB):
                tg = bb * TGB + tgb
                T0 = tg * 128
                xq = xq_[tgb]
                xw = [pass2_done]
                if xq_free[tgb] is not None:
                    xw = xw + xq_free[tgb]
                xcl = SY(lambda e, xq=xq, T0=T0: e.dma_start(xq, xs[T0:T0 + 128, :]),
                         waits=xw, inc=(s_x0 if tg % 2 == 0 else s_x1))
                wy = [pass2_done]
                if psT_free is not None:
                    wy.append(psT_free)
                lastT = None
                for g in range(G):
                    lastT = P_(lambda e, g=g, T0=T0: e.transpose(
                        psT[:, g * 128:(g + 1) * 128],
                        NXb[:, g * TOK + T0: g * TOK + T0 + 128], id_f[:, :]),
                        waits=wy if g == 0 else [])
                tpY_done[tg] = lastT
                ra = V(lambda e, xq=xq, tgb=tgb: e.scalar_tensor_tensor(
                    xq[:, 0:512], psT[:, 0:512], 1.0, xq[:, 0:512],
                    ALU.mult, ALU.add, accum_out=stQh[:, tgb * 2:tgb * 2 + 1]),
                    waits=[xcl, lastT])
                rb = V(lambda e, xq=xq, tgb=tgb: e.scalar_tensor_tensor(
                    xq[:, 512:1024], psT[:, 512:1024], 1.0, xq[:, 512:1024],
                    ALU.mult, ALU.add,
                    accum_out=stQh[:, tgb * 2 + 1:tgb * 2 + 2]),
                    waits=[xcl, lastT, ra])
                psT_free = rb
                s1l.append(V(lambda e, tg=tg, tgb=tgb: e.tensor_tensor(
                    stQ[:, tg:tg + 1], stQh[:, tgb * 2:tgb * 2 + 1],
                    stQh[:, tgb * 2 + 1:tgb * 2 + 2], ALU.add),
                    waits=[ra, rb]))
                zq = ZS_[bb % 2][:, tgb * D:(tgb + 1) * D]
                wsq = [ra, rb]
                if bb >= 2:
                    wsq.append(final_stt[(bb - 2) * TGB + tgb])
                    wsq.append(out_st[(bb - 2) * TGB + tgb])
                sq = V(lambda e, xq=xq, zq=zq, tg=tg: e.scalar_tensor_tensor(
                    zq, xq, 1.0, xq, ALU.mult, ALU.mult,
                    accum_out=stQ2[:, tg:tg + 1]), waits=wsq)
                sqpre_done[tg] = sq
                if STAGE == 41 and bb == 0 and tgb == 1:
                    pv = W.op("sync", lambda e: e.dma_start(
                        ys128[:, 0:1024], xq_[0]), [s1l[0], sqpre_done[0], sq,
                                                    s1l[1]], s_dbg, 16)
                    pv = W.op("sync", lambda e: e.dma_start(
                        ys128[:, 1024:2048], xq_[1]), [pv], s_dbg, 16)
                    pv = W.op("sync", lambda e: e.dma_start(
                        ys128[:, 2048:2052], stQ[:, 0:4]), [pv], s_dbg, 16)
                    pv = W.op("sync", lambda e: e.dma_start(
                        ys128[:, 2052:2056], stQ2[:, 0:4]), [pv], s_dbg, 16)
                    pv = W.op("sync", lambda e: e.dma_start(
                        ys128[:, 2056:2060], stQh[:, 0:4]), [pv], s_dbg, 16)
                    W.op("sync", lambda e: e.wait_ge(s_dbg, 5 * 16), [], None)
                    return True
            b4 = slice(bb * TGB, bb * TGB + TGB)
            w_ch = s1l + [sqpre_done[bb * TGB + t] for t in range(TGB)]
            cm = V(lambda e, b4=b4: e.tensor_scalar_mul(
                stM[:, b4], stQ[:, b4], 1.0 / D), waits=w_ch)
            cv = V(lambda e, b4=b4: e.tensor_scalar(
                stQ[:, b4], stQ2[:, b4], 1.0 / D, 1e-5, ALU.mult, ALU.add),
                waits=[cm])
            cm2 = V(lambda e, b4=b4: e.tensor_tensor(
                stQ2[:, b4], stM[:, b4], stM[:, b4], ALU.mult), waits=[cv])
            cv2 = V(lambda e, b4=b4: e.tensor_tensor(
                stQ[:, b4], stQ[:, b4], stQ2[:, b4], ALU.subtract), waits=[cm2])
            cl = A_(lambda e, b4=b4: e.activation(
                stQ2[:, b4], stQ[:, b4], AF.Ln), waits=[cv2])
            cr = A_(lambda e, b4=b4: e.activation(
                stR[:, b4], stQ2[:, b4], AF.Exp, scale=-0.5), waits=[cl])
            ln1_read[bb] = cv2
            for tgb in range(TGB):
                tg = bb * TGB + tgb
                xq3t = xq3[tgb]
                wxn = [cr, sqpre_done[tg], tpY_done[tg]]
                xn = V(lambda e, xq3t=xq3t, tg=tg: e.tensor_scalar(
                    ACC(tg), xq3t, stM[:, tg:tg + 1], stR[:, tg:tg + 1],
                    ALU.subtract, ALU.mult), waits=wxn)
                xn_done[tg] = xn
                xq_free[tgb] = [xn, sqpre_done[tg]]
                wcb = [xn]
                if tg >= 2:
                    wcb.append(lhsT_done[tg - 2])
                cb = A_(lambda e, tg=tg: e.copy(
                    xnb3[:, tg % 2, :, :], ACC(tg)), waits=wcb)
                wlb = [cb, c_idb]
                if psB_free is not None:
                    wlb.append(psB_free)
                lastB = None
                for g in range(G):
                    lastB = P_(lambda e, g=g, tg=tg: e.transpose(
                        psB[:, g * 128:(g + 1) * 128],
                        xnb[:, (tg % 2) * D + g * 128:(tg % 2) * D + (g + 1) * 128],
                        id_b[:, :]), waits=wlb if g == 0 else [])
                wlbe = [lastB]
                if bb >= 2:
                    wlbe.append(e7mm[bb - 2])
                    wlbe.append(f11_done[(bb - 2) * TGB + 3])
                lb = V(lambda e, tg=tg, tgb=tgb, bb=bb: e.tensor_copy(
                    lhsT_[bb % 2][:, tgb * D:(tgb + 1) * D], psB[:, :]), waits=wlbe)
                psB_free = lb
                lhsT_done[tg] = lb

        def emit_softmax_mix(bb):
            nonlocal psM_free
            ZR = ZS_[bb % 2]
            Z3 = ZR.rearrange("p (t d) -> p t d", d=D)
            w1 = [zev_done[bb * TGB + t] for t in range(TGB)]
            x1 = V(lambda e, Z3=Z3: e.reduce_max(nmz4[:, :], Z3, axis=AX.X,
                                                 negate=True), waits=w1)
            x2 = V(lambda e: e.tensor_scalar_mul(mzt4[:, :], nmz4[:, :], -1.0),
                   waits=[x1])
            x3l = [None] * TGB
            x4l = [None] * TGB
            for t in range(TGB):
                x3l[t] = V(lambda e, t=t, ZR=ZR: e.max_index(
                    idx4[:, t * 8:(t + 1) * 8],
                    mzt4[:, t:t + 1].broadcast_to([128, 8]),
                    ZR[:, t * D:(t + 1) * D]), waits=[x2] if t == 0 else [x2, x3l[t - 1]])
                x4l[t] = A_(lambda e, t=t, ZR=ZR: e.activation(
                    ZR[:, t * D:(t + 1) * D], ZR[:, t * D:(t + 1) * D], AF.Exp,
                    bias=nmz4[:, t:t + 1]), waits=[x3l[t]])
            x4b = V(lambda e, Z3=Z3: e.reduce_sum(ssum4[:, :], Z3, axis=AX.X),
                    waits=[x4l[TGB - 1]] + x4l[:TGB - 1])
            x5 = V(lambda e: e.reciprocal(p4[:, :], ssum4[:, :]), waits=[x4b])
            idv = idx4.ap().rearrange("p (t e) -> p t e", e=8)
            e8v = e8u4.ap().rearrange("p (t o) -> p t o", o=1)
            x7 = V(lambda e, idv=idv, e8v=e8v: e.tensor_scalar(
                e8v, idv[:, :, 0:1], 7, None, ALU.bitwise_and),
                waits=[x3l[TGB - 1]] + x3l[:TGB - 1])
            x8 = V(lambda e: e.tensor_copy(e8f4[:, :], e8u4[:, :]), waits=[x7])
            for t in range(TGB):
                tg = bb * TGB + t
                m1 = V(lambda e, t=t: e.tensor_tensor(
                    me8[:, :], e8f4[:, t:t + 1].broadcast_to([128, 8]), iota[:, :],
                    ALU.is_equal), waits=[x8, x5, ld_all] if t == 0 else [psM_free])
                m2 = V(lambda e, t=t, bb=bb: e.tensor_scalar(
                    alpha[:, (bb % 2) * 32 + t * 8:(bb % 2) * 32 + (t + 1) * 8],
                    me8[:, :], p4[:, t:t + 1], None, ALU.mult), waits=[m1])
                tpA = P_(lambda e: e.transpose(
                    psM[0:8, 0:128], me8[:, :], id_f[:, :]), waits=[m1, psM_free])
                evA = V(lambda e, t=t: e.tensor_copy(
                    alphaT[0:8, t * 128:(t + 1) * 128], psM[0:8, 0:128]),
                    waits=[tpA, m2])
                mixe = None
                for hf in range(2):
                    mxa = P_(lambda e, t=t, hf=hf: e.matmul(
                        psM[:, :], alphaT[0:8, t * 128:(t + 1) * 128],
                        behi16[0:8, hf * 512:(hf + 1) * 512], start=True, stop=False),
                        waits=[evA, be_ready] if hf == 0 else [mixe])
                    mx = P_(lambda e, t=t, hf=hf: e.matmul(
                        psM[:, :], alphaT[0:8, t * 128:(t + 1) * 128],
                        belo16[0:8, hf * 512:(hf + 1) * 512], start=False, stop=True),
                        waits=[])
                    wmx = [mx, x4b, x5]
                    if bb >= 2:
                        wmx.append(final_stt[(bb - 2) * TGB + t])
                    mixe = A_(lambda e, t=t, hf=hf, bb=bb: e.activation(
                        ZS_[bb % 2][:, t * D + hf * 512: t * D + (hf + 1) * 512],
                        psM[:, :], AF.Identity, scale=p4[:, t:t + 1]), waits=wmx)
                psM_free = mixe
                mixevac[tg] = mixe
            alpha_last = mixevac[bb * TGB + TGB - 1]

        def emit_ln2(b):
            b4 = slice(b * TGB, b * TGB + TGB)
            for tgb in range(TGB):
                tg = b * TGB + tgb
                zq3 = ZS_[b % 2][:, tgb * D:(tgb + 1) * D].rearrange(
                    "p (g t) -> p g t", t=128)
                fs = V(lambda e, zq3=zq3, tg=tg: e.scalar_tensor_tensor(
                    ACC(tg), zq3, 0.0, ACC(tg), ALU.max, ALU.add,
                    accum_out=stQ[:, tg:tg + 1]), waits=[selstt[tg]])
                final_stt[tg] = fs
                scr3 = scrapL[b % 2][:, (tgb % 2) * D:(tgb % 2 + 1) * D].rearrange(
                    "p (g t) -> p g t", t=128)
                sq = V(lambda e, scr3=scr3, tg=tg: e.scalar_tensor_tensor(
                    scr3, ACC(tg), 1.0, ACC(tg), ALU.mult, ALU.mult,
                    accum_out=stQ2[:, tg:tg + 1]), waits=[fs, e7mm[b]])
                final_stt[tg] = sq
            w_ch = [final_stt[b * TGB + t] for t in range(TGB)]
            cm = V(lambda e, b4=b4: e.tensor_scalar_mul(
                stM[:, b4], stQ[:, b4], 1.0 / D), waits=w_ch)
            cv = V(lambda e, b4=b4: e.tensor_scalar(
                stQ[:, b4], stQ2[:, b4], 1.0 / D, 1e-5, ALU.mult, ALU.add),
                waits=[cm])
            cm2 = V(lambda e, b4=b4: e.tensor_tensor(
                stQ2[:, b4], stM[:, b4], stM[:, b4], ALU.mult), waits=[cv])
            cv2 = V(lambda e, b4=b4: e.tensor_tensor(
                stQ[:, b4], stQ[:, b4], stQ2[:, b4], ALU.subtract), waits=[cm2])
            cl = A_(lambda e, b4=b4: e.activation(
                stQ2[:, b4], stQ[:, b4], AF.Ln), waits=[cv2])
            cr = A_(lambda e, b4=b4: e.activation(
                stR[:, b4], stQ2[:, b4], AF.Exp, scale=-0.5), waits=[cl])
            c7 = V(lambda e, b4=b4: e.tensor_tensor(
                stQ[:, b4], stM[:, b4], stR[:, b4], ALU.mult), waits=[cr])
            c8 = V(lambda e, b4=b4: e.tensor_scalar_mul(
                stQ[:, b4], stQ[:, b4], -1.0), waits=[c7])
            for tgb in range(TGB):
                tg = b * TGB + tgb
                zq = ZS_[b % 2][:, tgb * D:(tgb + 1) * D]
                zq3 = zq.rearrange("p (g t) -> p g t", t=128)
                f11 = A_(lambda e, tg=tg, zq3=zq3: e.activation(
                    zq3, ACC(tg), AF.Identity,
                    scale=stR[:, tg:tg + 1], bias=stQ[:, tg:tg + 1]),
                    waits=[c8, final_stt[tg]])
                f11_done[tg] = f11
                out_st[tg] = SY(lambda e, tg=tg, zq=zq: e.dma_start(
                    ys[tg * 128:(tg + 1) * 128, :], zq[:, :]), waits=[f11],
                    inc=(s_o0 if tg % 2 == 0 else s_o1))

        def emit_wload(gi):
            if wready[gi] is not None:
                return
            key = wgroups[gi]
            slot = We_[gi % 2]
            base_w = []
            if gi >= 2:
                base_w.append(glast_mm[gi - 2])
                if wb_done[gi - 2] is not None:
                    base_w.append(wb_done[gi - 2])
            if key[0] == "Wg":
                if key[1] == 0:
                    wready[gi] = [wg_cast_done]
                    return
                h0 = GW(lambda e, slot=slot: e.dma_start(
                    slot[:, 0:4 * D], wg16[:, 0:4 * D]),
                    waits=base_w + [wg_wb], inc=(s_we0 if gi % 2 == 0 else s_we1))
                h1 = SY(lambda e, slot=slot: e.dma_start(
                    slot[:, 4 * D:8 * D], wg16[:, 4 * D:8 * D]),
                    waits=base_w + [wg_wb], inc=s_w1)
                wready[gi] = [h0, h1]
            elif key[1] == 0:
                ei = key[2]
                prevc = [None, None]
                lastc = None
                for g in range(G):
                    if prevc[g % 2] is not None:
                        wd = [prevc[g % 2]]
                    else:
                        wd = list(base_w) + [wg_wb, cast_jobs["last_cast"], tp_last[3]]
                    if g % 2 == 0:
                        dd = GW(lambda e, ei=ei, g=g: e.dma_start(
                            stg_[g % 2], Wed[ei, g * 128:(g + 1) * 128, :]), waits=wd,
                            inc=s_w0)
                        lastc = A_(lambda e, g=g, slot=slot: e.copy(
                            slot[:, g * D:(g + 1) * D], stg_[g % 2]), waits=[dd])
                    else:
                        dd = SY(lambda e, ei=ei, g=g: e.dma_start(
                            stg_[g % 2], Wed[ei, g * 128:(g + 1) * 128, :]), waits=wd,
                            inc=s_w1)
                        lastc = V(lambda e, g=g, slot=slot: e.tensor_copy(
                            slot[:, g * D:(g + 1) * D], stg_[g % 2]), waits=[dd])
                    prevc[g % 2] = lastc
                cast_jobs["last_cast"] = lastc
                wready[gi] = [lastc]
                wb_done[gi] = SY(lambda e, ei=ei, slot=slot: e.dma_start(
                    we16[ei, :, :], slot[:, :]), waits=[prevc[0], prevc[1]], inc=s_wb)
            else:
                ei = key[2]
                base_w.append(wb_done[seen[("E", 0, ei)]])
                h0 = GW(lambda e, slot=slot, ei=ei: e.dma_start(
                    slot[:, 0:4 * D], we16[ei, :, 0:4 * D]), waits=base_w,
                    inc=(s_we0 if gi % 2 == 0 else s_we1))
                h1 = SY(lambda e, slot=slot, ei=ei: e.dma_start(
                    slot[:, 4 * D:8 * D], we16[ei, :, 4 * D:8 * D]), waits=base_w,
                    inc=s_w1)
                wready[gi] = [h0, h1]

        emit_wload(0)
        if emit_prepass(0):
            return
        emit_wload(1)

        if STAGE == 42:
            pv = W.op("sync", lambda e: e.dma_start(ys128[:, 0:12288],
                      NXb.ap()[:, 0:12288]),
                      [xn_done[t] for t in range(4)]
                      + [lhsT_done[t] for t in range(4)], s_dbg, 16)
            pv = W.op("sync", lambda e: e.dma_start(
                ys128[:, 12288:12292], stM[:, 0:4]), [pv], s_dbg, 16)
            pv = W.op("sync", lambda e: e.dma_start(
                ys128[:, 12292:12296], stR[:, 0:4]), [pv], s_dbg, 16)
            pv = W.op("sync", lambda e: e.dma_start(
                ys128[:, 12296:12300], stQ[:, 0:4]), [pv], s_dbg, 16)
            pv = W.op("sync", lambda e: e.dma_start(
                ys128[:, 12300:12304], stQ2[:, 0:4]), [pv], s_dbg, 16)
            W.op("sync", lambda e: e.wait_ge(s_dbg, 5 * 16), [], None)
            return

        for ji, job in enumerate(jobs):
            gi = jgroup[ji]
            tile = PS[ji % 2]
            if job[0] == "G":
                bb, tgb = job[1], job[2]
                tg = bb * TGB + tgb
                wm = list(wready[gi]) + [lhsT_done[tg]]
                if psfree[ji % 2] is not None:
                    wm.append(psfree[ji % 2])
                lastmm = None
                for g in range(G):
                    for bank in range(2):
                        lastmm = P_(lambda e, tile=tile, g=g, bank=bank, tgb=tgb, bb=bb:
                                    e.matmul(
                            tile[:, bank * 512:(bank + 1) * 512],
                            lhsT_[bb % 2][:, tgb * D + g * 128: tgb * D + (g + 1) * 128],
                            We_[gi % 2][:, g * D + bank * 512: g * D + bank * 512 + 512],
                            start=(g == 0), stop=(g == G - 1)),
                            waits=wm if (g == 0 and bank == 0) else [])
                glast_mm[gi] = lastmm
                zq = ZS_[bb % 2][:, tgb * D:(tgb + 1) * D]
                wz = [lastmm, bg_ready, sqpre_done[tg]]
                if bb >= 2:
                    wz.append(final_stt[(bb - 2) * TGB + tgb])
                zev_done[tg] = V(lambda e, zq=zq, tile=tile: e.scalar_tensor_tensor(
                    zq, tile[:, :], 1.0, BGfull[:, :], ALU.mult, ALU.add), waits=wz)
                psfree[ji % 2] = zev_done[tg]
                if tgb == TGB - 1 and bb == 0:
                    if STAGE == 43:
                        W.op("sync", lambda e: e.dma_start(
                            ys128[:, 0:4096], ZS_[0][:, :]),
                            [zev_done[t] for t in range(4)], s_dbg, 16)
                        W.op("sync", lambda e: e.wait_ge(s_dbg, 16), [], None)
                        return
                    emit_softmax_mix(bb)
                    if STAGE == 44:
                        pv = W.op("sync", lambda e: e.dma_start(
                            ys128[:, 0:4096], ZS_[0][:, :]),
                            [mixevac[t] for t in range(4)], s_dbg, 16)
                        pv = W.op("sync", lambda e: e.dma_start(
                            ys128[:, 4096:4128], alpha[:, 0:32]), [pv], s_dbg, 16)
                        pv = W.op("sync", lambda e: e.dma_start(
                            ys128[:, 4128:4132], p4[:, :]), [pv], s_dbg, 16)
                        pv = W.op("sync", lambda e: e.dma_start(
                            ys128[:, 4132:4136], e8f4[:, :]), [pv], s_dbg, 16)
                        W.op("sync", lambda e: e.wait_ge(s_dbg, 64), [], None)
                        return
            else:
                b, ei, tgb = job[1], job[2], job[3]
                tg = b * TGB + tgb
                wm = list(wready[gi]) + [lhsT_done[tg]]
                if psfree[ji % 2] is not None:
                    wm.append(psfree[ji % 2])
                lastmm = None
                for g in range(G):
                    for bank in range(2):
                        lastmm = P_(lambda e, tile=tile, g=g, bank=bank, tgb=tgb, b=b, gi=gi:
                                    e.matmul(
                            tile[:, bank * 512:(bank + 1) * 512],
                            lhsT_[b % 2][:, tgb * D + g * 128: tgb * D + (g + 1) * 128],
                            We_[gi % 2][:, g * D + bank * 512: g * D + bank * 512 + 512],
                            start=(g == 0), stop=(g == G - 1)),
                            waits=wm if (g == 0 and bank == 0) else [])
                glast_mm[gi] = lastmm
                if ei == G - 1 and tgb == TGB - 1:
                    e7mm[b] = lastmm
                zq = ZS_[b % 2][:, tgb * D:(tgb + 1) * D]
                prev_acc = mixevac[tg] if ei == 0 else selstt[tg]
                ss = V(lambda e, zq=zq, tile=tile, b=b, tg=tg, tgb=tgb, ei=ei:
                       e.scalar_tensor_tensor(
                    zq, tile[:, :],
                    alpha[:, (b % 2) * 32 + tgb * 8 + ei:(b % 2) * 32 + tgb * 8 + ei + 1],
                    zq, ALU.mult, ALU.add), waits=[lastmm, prev_acc])
                selstt[tg] = ss
                psfree[ji % 2] = ss
                if ei == 1 and tgb == TGB - 1 and b + 1 < NB:
                    emit_prepass(b + 1)
                if ei == 4 and tgb == TGB - 1 and b + 1 < NB:
                    emit_softmax_mix(b + 1)
                if ei == G - 1 and tgb == TGB - 1:
                    emit_ln2(b)
            if ji + 1 < len(jobs) and jgroup[ji + 1] != gi:
                nxt = gi + 2
                if nxt < len(wgroups):
                    emit_wload(nxt)

        SY(lambda e: e.wait_ge(s_o0, 16 * (NTG // 2)), waits=[], inc=None)
        SY(lambda e: e.wait_ge(s_o1, 16 * (NTG // 2)), waits=[], inc=None)

    cnt = Walk("count")
    schedule(cnt)

    with nc.Block() as block:
        for key, starter in [("sync", block.sync), ("vector", block.vector),
                             ("scalar", block.scalar), ("tensor", block.tensor),
                             ("gpsimd", block.gpsimd)]:
            def body(eng, key=key):
                w = Walk(key, cnt.events)
                w.eng = eng
                schedule(w)
            starter(body)

    stack.close()
    return nc


_NC_CACHE = {}


def kernel(L0_attn_x, L0_attn_a_param, L0_attn_w_in, L0_attn_b_in, L0_attn_w_a,
           L0_attn_b_a, L0_ffn_Wg, L0_ffn_bg, L0_ffn_We, L0_ffn_be):
    f32 = np.float32
    x = np.ascontiguousarray(np.asarray(L0_attn_x, dtype=f32))
    B, L, Dm = x.shape
    shared = dict(
        win=np.ascontiguousarray(np.asarray(L0_attn_w_in, f32)),
        wa=np.ascontiguousarray(np.asarray(L0_attn_w_a, f32)),
        bin_t=np.ascontiguousarray(np.asarray(L0_attn_b_in, f32).T),
        ba_t=np.ascontiguousarray(np.asarray(L0_attn_b_a, f32).T),
        apar_t=np.ascontiguousarray(np.asarray(L0_attn_a_param, f32).reshape(G, 128).T),
        Wgd=np.ascontiguousarray(np.asarray(L0_ffn_Wg, f32)),
        Wed=np.ascontiguousarray(np.asarray(L0_ffn_We, f32)),
        bgd=np.ascontiguousarray(np.asarray(L0_ffn_bg, f32).reshape(1, Dm)),
        bed=np.ascontiguousarray(np.asarray(L0_ffn_be, f32)),
        ident=np.eye(128, dtype=f32),
        iot8=np.ascontiguousarray(np.tile(np.arange(8, dtype=f32), (128, 1))),
    )
    in_maps = []
    for c in range(8):
        b, k = c // 4, c % 4
        m = dict(shared)
        m["xs"] = np.ascontiguousarray(x[b, k * TOK:(k + 1) * TOK, :])
        ms = np.zeros((8, 8), f32)
        for j in range(8):
            if j // 4 == b and j % 4 < k:
                ms[:, j] = 1.0
        m["msel"] = ms
        in_maps.append(m)

    if "nc" not in _NC_CACHE:
        _NC_CACHE["nc"] = build_program()
    nc = _NC_CACHE["nc"]
    res = run_bass_kernel_spmd(nc, in_maps, core_ids=list(range(8)))
    out = np.empty((B, L, Dm), dtype=np.float32)
    for c in range(8):
        b, k = c // 4, c % 4
        out[b, k * TOK:(k + 1) * TOK, :] = res.results[c]["ys"]
    return out
